# version 11
# baseline (speedup 1.0000x reference)
import math
from contextlib import ExitStack

import numpy as np

import concourse.bass as bass
import concourse.mybir as mybir
from concourse.bass_utils import run_bass_kernel_spmd

F32 = mybir.dt.float32
BF16 = mybir.dt.bfloat16
U8 = mybir.dt.uint8
AF = mybir.ActivationFunctionType
OP = mybir.AluOpType

LAMBDA_INIT = 0.8 - 0.6 * math.exp(-0.3 * 12)
EPS = 1e-5

S, E, I, D, DV = 2048, 1024, 2816, 64, 128
NT, EC, IC = 16, 8, 22
NV, QB, NQB, KC = 8, 512, 4, 16
NFTB = 4
ARN_N = 84000
ARF_N = 8400
ENGINES = ("sync", "tensor", "vector", "scalar", "gpsimd")
SEM_NAMES = ("dS", "dG", "TE", "AC", "VE", "PL")


class EM:

    def __init__(self):
        self.ev = {e: [] for e in ENGINES}
        self.cnt = {s: 0 for s in SEM_NAMES}
        self.chk = {"dS": set(), "dG": set()}

    ENG_SEM = {"vector": "VE", "scalar": "AC", "gpsimd": "PL"}

    def op(self, eng, fn, waits=(), inc=None, k=1):
        ws = [(s, v) for (s, v) in waits if v > 0]
        es = self.ENG_SEM.get(eng)
        if es is not None:
            if inc is None:
                inc = es
            if self.cnt[es] > 0:
                ws.append((es, self.cnt[es]))
        for (sm, v) in ws:
            if sm in self.chk:
                self.chk[sm].add(v)
        if inc is not None:
            self.cnt[inc] += k
        cum = self.cnt[inc] if inc else None
        self.ev[eng].append((fn, tuple(ws), inc, k, cum))
        return cum

    def val(self, s):
        return self.cnt[s]


def build_program():
    nc = bass.Bass("TRN2", target_bir_lowering=False)

    x_d = nc.dram_tensor("x", [2, S, E], F32, kind="ExternalInput")
    wq_d = nc.dram_tensor("wq", [E, E], F32, kind="ExternalInput")
    wk_d = nc.dram_tensor("wk", [E, E], F32, kind="ExternalInput")
    wv_d = nc.dram_tensor("wv", [E, E], F32, kind="ExternalInput")
    wo_d = nc.dram_tensor("wo", [E, E], F32, kind="ExternalInput")
    wg_d = nc.dram_tensor("wg", [I, E], F32, kind="ExternalInput")
    wu_d = nc.dram_tensor("wu", [I, E], F32, kind="ExternalInput")
    wd_d = nc.dram_tensor("wd", [E, I], F32, kind="ExternalInput")
    wmod_d = nc.dram_tensor("wmod", [E, 6 * E], F32, kind="ExternalInput")
    cv_d = nc.dram_tensor("cv", [2, E], F32, kind="ExternalInput")
    lq1_d = nc.dram_tensor("lq1", [1, D], F32, kind="ExternalInput")
    lk1_d = nc.dram_tensor("lk1", [1, D], F32, kind="ExternalInput")
    lq2_d = nc.dram_tensor("lq2", [1, D], F32, kind="ExternalInput")
    lk2_d = nc.dram_tensor("lk2", [1, D], F32, kind="ExternalInput")
    subw_d = nc.dram_tensor("subw", [DV, 1], F32, kind="ExternalInput")
    y_d = nc.dram_tensor("y", [2, S, E], F32, kind="ExternalOutput")
    x1_d = nc.dram_tensor("x1s", [2, S, E], F32)
    wgT_d = nc.dram_tensor("wgTs", [E, I], BF16)
    wuT_d = nc.dram_tensor("wuTs", [E, I], BF16)
    wdT_d = nc.dram_tensor("wdTs", [I, E], BF16)
    modr_d = nc.dram_tensor("modrs", [1, 2 * 6144], BF16)

    em = EM()
    op = em.op
    V = em.val

    ctx = ExitStack()
    with ctx:
        en = ctx.enter_context
        ARN = en(nc.sbuf_tensor("ARN", [128, ARN_N], BF16))
        ARF = en(nc.sbuf_tensor("ARF", [128, ARF_N], F32))
        ARU = en(nc.sbuf_tensor("ARU", [128, 1024], U8))
        BCS = en(nc.sbuf_tensor("BCS", [128, 16], F32))
        pb = [en(nc.psum_tensor(f"pb{i}", [128, 1024], F32)) for i in range(4)]
        SEM = {s: en(nc.semaphore(s)) for s in SEM_NAMES}

        cur = [0]

        def nalloc(n):
            o = cur[0]
            cur[0] += n
            assert cur[0] <= ARN_N, f"ARN overflow {cur[0]}"
            return o

        O_vrep = nalloc(6 * 1024)
        O_hst = nalloc(4 * 1024)
        O_sml = nalloc(1664)
        OV = cur[0]
        A_hT = OV
        A_E0 = A_hT + 8 * 2048
        A_E1 = A_E0 + 16 * 512
        A_attnT = A_E1 + 16 * 512
        A_QK = A_attnT + 8 * 2048
        A_Vt = A_QK + 4 * 2048
        A_attq = A_Vt + 16 * 144
        A_vfe = A_attq + 2 * 4 * 128
        A_wqkv = A_vfe + 2 * 512
        A_wo = A_wqkv + 3 * 2 * 1024
        A_end = A_wo + 2 * 1024
        assert A_end <= ARN_N, A_end
        O_modr = A_E0
        B_xqT = OV
        B_sgmqT = B_xqT + 8 * 512
        B_mid = B_sgmqT + 22 * 512
        B_mtok = B_mid + 22 * 512
        B_T1 = B_mtok + 2 * 2816
        B_T2 = B_T1 + 2816
        B_T3 = B_T2 + 2816
        B_T4 = B_T3 + 2816
        B_mq = B_T4 + 2816
        B_dfe = B_mq + 2816
        B_h2 = B_dfe + 8 * 512
        B_xq = B_h2 + 1024
        B_wgu = B_xq + 1024
        B_wd = B_wgu + 2 * 4 * 1024
        B_wd2 = B_wd + 22 * 128
        B_end = B_wd2 + 22 * 128
        assert B_end <= ARN_N, B_end

        fc = [0]

        def falloc(n):
            o = fc[0]
            fc[0] += n
            assert fc[0] <= ARF_N, f"ARF overflow {fc[0]}"
            return o

        F_xt = falloc(2 * 1024)
        F_ab = falloc(2 * 4 * 132)
        F_bn = falloc(16)
        F_st = falloc(8)
        F_lam = falloc(8)
        F_wf = falloc(1)
        F_eps = falloc(1)
        F_one = falloc(1)
        F_bm = falloc(176)
        F_ss = falloc(176)
        F_rc = falloc(176)
        F_cmb = falloc(16)
        F_sq = falloc(8)
        F_wfp = falloc(2816)

        def nview(off, *shape):
            tot = int(np.prod(shape))
            ap = ARN[:, off:off + tot]
            if len(shape) > 1:
                names = " ".join(f"d{i}" for i in range(len(shape)))
                ap = ap.rearrange(f"p ({names}) -> p {names}",
                                  **{f"d{i}": s for i, s in enumerate(shape)})
            return ap

        def fview(off, *shape):
            tot = int(np.prod(shape))
            ap = ARF[:, off:off + tot]
            if len(shape) > 1:
                names = " ".join(f"d{i}" for i in range(len(shape)))
                ap = ap.rearrange(f"p ({names}) -> p {names}",
                                  **{f"d{i}": s for i, s in enumerate(shape)})
            return ap

        def nap(off, aplist):
            return bass.AP(ARN, off, [[ARN_N, 128]] + aplist)

        def dma(eng, out_ap, in_ap, waits=(), transpose=False, slow=False):
            sem = "dG" if eng == "gpsimd" else "dS"
            if transpose:
                fn = lambda e, o=out_ap, i=in_ap: e.dma_start(o, i, transpose=True)
            elif slow:
                fn = lambda e, o=out_ap, i=in_ap: e.dma_start(
                    o, i, allow_slow_non_contiguous=True)
            else:
                fn = lambda e, o=out_ap, i=in_ap: e.dma_start(o, i)
            return op(eng, fn, waits, sem, 16)

        bankg = {}

        def bank_wait(key):
            return [bankg[key]] if key in bankg else []

        def bank_release(key, sem, val):
            bankg[key] = (sem, val)

        op("vector", lambda v: v.memset(ARF[:, F_eps:F_eps + 1], EPS), inc="VE")
        op("vector", lambda v: v.memset(ARF[:, F_one:F_one + 1], 1.0), inc="VE")
        dma("sync", ARF[:, F_wf:F_wf + 1], subw_d[:, :])
        op("vector", lambda v: v.tensor_scalar(
            ARF[:, F_wf:F_wf + 1], ARF[:, F_wf:F_wf + 1], 1.0 - LAMBDA_INIT, None, OP.mult),
            waits=[("dS", V("dS"))], inc="VE")

        lo = F_wfp
        dma("sync", ARF[0:1, lo:lo + 64], lq1_d[:, :])
        dma("sync", ARF[0:1, lo + 64:lo + 128], lk1_d[:, :])
        dma("sync", ARF[0:1, lo + 128:lo + 192], lq2_d[:, :])
        dma("sync", ARF[0:1, lo + 192:lo + 256], lk2_d[:, :])
        op("vector", lambda v: v.tensor_tensor(
            ARF[0:1, lo + 256:lo + 320], ARF[0:1, lo:lo + 64],
            ARF[0:1, lo + 64:lo + 128], OP.mult), waits=[("dS", V("dS"))], inc="VE")
        op("vector", lambda v: v.tensor_tensor(
            ARF[0:1, lo + 320:lo + 384], ARF[0:1, lo + 128:lo + 192],
            ARF[0:1, lo + 192:lo + 256], OP.mult), inc="VE")
        op("vector", lambda v: v.reduce_sum(
            ARF[0:1, F_lam:F_lam + 1], ARF[0:1, lo + 256:lo + 320],
            axis=mybir.AxisListType.X), waits=[("VE", V("VE"))], inc="VE")
        op("vector", lambda v: v.reduce_sum(
            ARF[0:1, F_lam + 1:F_lam + 2], ARF[0:1, lo + 320:lo + 384],
            axis=mybir.AxisListType.X), inc="VE")
        op("scalar", lambda s: s.activation(
            ARF[0:1, F_lam + 2:F_lam + 3], ARF[0:1, F_lam:F_lam + 1], AF.Exp),
            waits=[("VE", V("VE"))], inc="AC")
        op("scalar", lambda s: s.activation(
            ARF[0:1, F_lam + 3:F_lam + 4], ARF[0:1, F_lam + 1:F_lam + 2], AF.Exp), inc="AC")
        op("vector", lambda v: v.tensor_tensor(
            ARF[0:1, F_lam + 4:F_lam + 5], ARF[0:1, F_lam + 2:F_lam + 3],
            ARF[0:1, F_lam + 3:F_lam + 4], OP.subtract),
            waits=[("AC", V("AC"))], inc="VE")
        op("vector", lambda v: v.tensor_scalar(
            ARF[0:1, F_lam + 4:F_lam + 5], ARF[0:1, F_lam + 4:F_lam + 5], 1.0, LAMBDA_INIT,
            OP.mult, OP.add), waits=[("VE", V("VE"))], inc="VE")
        op("vector", lambda v: v.memset(BCS[:, 2:3], 1.0 / 12.0), inc="VE")
        op("vector", lambda v: v.tensor_scalar(
            BCS[:, 3:4], BCS[:, 2:3], 12.0, None, OP.mult),
            waits=[("VE", V("VE"))], inc="VE")
        op("vector", lambda v: v.tensor_scalar(
            BCS[:, 3:4], BCS[:, 3:4], -1.0, 2.0, OP.mult, OP.add), inc="VE")
        op("vector", lambda v: v.tensor_tensor(
            BCS[:, 2:3], BCS[:, 2:3], BCS[:, 3:4], OP.mult),
            waits=[("VE", V("VE"))], inc="VE")
        op("vector", lambda v: v.tensor_copy(BCS[0:1, 0:1], ARF[0:1, F_lam + 4:F_lam + 5]),
           waits=[("VE", V("VE"))], inc="VE")
        dma("sync", BCS[:, 1:2],
            bass.AP(BCS, 0, [[1, 1], [0, 128], [1, 1]]), waits=[("VE", V("VE"))])
        lamrep = BCS[:, 1:2]

        csil = nview(O_sml + 128, 8, 2)
        O_cT = O_sml + 144
        O_cb = O_sml + 272
        op("vector", lambda v: v.memset(ARN[0:16, O_cb:O_cb + 1024], 0.0), inc="VE")
        dma("gpsimd", ARN[0:2, O_cb:O_cb + 1024], cv_d[:, :], waits=[("VE", V("VE"))])
        dma("sync", bass.AP(ARN, O_cT, [[ARN_N, 128], [16, 8], [1, 16]]),
            ARN[0:16, O_cb:O_cb + 1024],
            waits=[("dG", V("dG")), ("VE", V("VE"))], transpose=True)
        csil_ac = op("scalar", lambda s: s.activation(
            csil, bass.AP(ARN, O_cT, [[ARN_N, 128], [16, 8], [1, 2]]), AF.Silu),
            waits=[("dS", V("dS"))], inc="AC")

        mslot_guard = {}
        for ns in range(12):
            sl = ns % 2
            boffs = A_hT + sl * 4096
            w = [mslot_guard[sl]] if sl in mslot_guard else []
            for kcc in range(EC):
                dma("gpsimd", nview(boffs + kcc * 512, 512),
                    bass.AP(wmod_d, kcc * 128 * 6144 + ns * 512, [[6144, 128], [1, 512]]),
                    waits=w if kcc == 0 else ())
            wdg = V("dG")
            for b in range(2):
                pskey = ("pb0", b)
                ps = pb[0][0:1, b * 512:b * 512 + 512]
                t = None
                for kcc in range(EC):
                    t = op("tensor", lambda t_, b=b, k=kcc, ps=ps, bo=boffs: t_.matmul(
                        ps, csil[:, k, b:b + 1], nview(bo + k * 512, 512),
                        start=(k == 0), stop=(k == EC - 1)),
                        waits=([("dG", wdg), ("AC", csil_ac)] + bank_wait(pskey))
                        if kcc == 0 else (),
                        inc="TE" if kcc == EC - 1 else None)
                vv = op("vector", lambda v, b=b, ns=ns, ps=ps: v.tensor_copy(
                    ARN[0:1, O_modr + b * 6144 + ns * 512:O_modr + b * 6144 + (ns + 1) * 512],
                    ps), waits=[("TE", t)], inc="VE")
                bank_release(pskey, "VE", vv)
            mslot_guard[sl] = ("TE", V("TE"))
        for b in range(2):
            base = O_modr + b * 6144
            for j in (1, 4):
                op("vector", lambda v, ba=base, j=j: v.tensor_scalar(
                    ARN[0:1, ba + j * 1024:ba + (j + 1) * 1024],
                    ARN[0:1, ba + j * 1024:ba + (j + 1) * 1024], 1.0, 1.0, OP.mult, OP.add),
                    waits=[("VE", V("VE"))], inc="VE")
            for j in (2, 5):
                mg = ARN[0:1, base + j * 1024:base + (j + 1) * 1024]
                t1 = ARN[0:1, B_T1:B_T1 + 1024]
                t2 = ARN[0:1, B_T2:B_T2 + 1024]
                op("vector", lambda v, m=mg, a=t1: v.tensor_scalar(
                    a, m, 0.0, 2.0, OP.is_ge, OP.mult),
                    waits=[("VE", V("VE")), ("AC", V("AC"))], inc="VE")
                op("vector", lambda v, a=t1: v.tensor_scalar(
                    a, a, -1.0, None, OP.add), waits=[("VE", V("VE"))], inc="VE")
                op("scalar", lambda s, m=mg, b2=t2: s.activation(b2, m, AF.Abs),
                   waits=[("VE", V("VE"))], inc="AC")
                op("vector", lambda v, b2=t2: v.tensor_scalar(
                    b2, b2, 0.25, None, OP.max),
                    waits=[("AC", V("AC"))], inc="VE")
                op("vector", lambda v, m=mg, a=t1, b2=t2: v.tensor_tensor(
                    m, a, b2, OP.mult), waits=[("VE", V("VE"))], inc="VE")
        dma("sync", modr_d[:, :], ARN[0:1, O_modr:O_modr + 2 * 6144],
            waits=[("VE", V("VE"))])
        prep_dma_done = V("dS")

        def quant_generic(src, width, nb, out_ap, pre_waits, t1o=None,
                          t1war=None, out_war=(), tmpo=None):
            nbv = nb
            t2o, t3o, t4o = tmpo if tmpo is not None else (B_T2, B_T3, B_T4)
            T1 = nview(t1o, width) if t1o is not None \
                else nview(B_T1, 2816)[:, 0:width]
            T2 = nview(t2o, width)
            T3 = nview(t3o, width)
            T4 = nview(t4o, width)
            srcb = src.rearrange("p (b k) -> p b k", k=16)
            red_v = op("vector", lambda v: v.reduce_max(
                ARF[:, F_bm:F_bm + nbv], srcb, axis=mybir.AxisListType.X,
                apply_absolute_value=True),
                waits=list(pre_waits) + [("VE", V("VE"))], inc="VE")
            op("vector", lambda v: v.tensor_scalar(
                ARF[:, F_bm:F_bm + nbv], ARF[:, F_bm:F_bm + nbv], 6e-12, None,
                OP.max), inc="VE")
            op("vector", lambda v: v.reciprocal(
                ARF[:, F_rc:F_rc + nbv], ARF[:, F_bm:F_bm + nbv]),
                waits=[("VE", V("VE"))], inc="VE")
            op("vector", lambda v: v.tensor_tensor(
                ARF[:, F_ss:F_ss + nbv], ARF[:, F_bm:F_bm + nbv],
                ARF[:, F_rc:F_rc + nbv], OP.mult), waits=[("VE", V("VE"))], inc="VE")
            op("vector", lambda v: v.tensor_scalar(
                ARF[:, F_ss:F_ss + nbv], ARF[:, F_ss:F_ss + nbv], -1.0, 2.0,
                OP.mult, OP.add), inc="VE")
            op("vector", lambda v: v.tensor_tensor(
                ARF[:, F_rc:F_rc + nbv], ARF[:, F_rc:F_rc + nbv],
                ARF[:, F_ss:F_ss + nbv], OP.mult), waits=[("VE", V("VE"))], inc="VE")
            op("vector", lambda v: v.tensor_scalar(
                ARF[:, F_rc:F_rc + nbv], ARF[:, F_rc:F_rc + nbv], 6.0, None,
                OP.mult), inc="VE")
            op("vector", lambda v: v.tensor_scalar(
                ARF[:, F_ss:F_ss + nbv], ARF[:, F_bm:F_bm + nbv], BCS[:, 2:3], None,
                OP.mult), inc="VE")
            a_abs = op("scalar", lambda s: s.activation(T1, src, AF.Abs),
                       waits=list(pre_waits)
                       + (list(t1war) if t1war is not None
                          else [("VE", V("VE"))]), inc="AC")
            rcb = bass.AP(ARF, F_rc, [[ARF_N, 128], [1, nbv], [0, 16]])
            op("vector", lambda v: v.tensor_tensor(
                T2.rearrange("p (b k) -> p b k", k=16),
                T1.rearrange("p (b k) -> p b k", k=16), rcb, OP.mult),
                waits=[("AC", a_abs), ("VE", V("VE"))], inc="VE")
            a_sgn = op("scalar", lambda s: s.activation(T1, src, AF.Sign),
                       waits=[("VE", V("VE"))], inc="AC")
            op("vector", lambda v: v.tensor_scalar(T3, T2, 0.25, None, OP.is_gt),
               waits=[("VE", V("VE"))], inc="VE")
            op("vector", lambda v: v.tensor_scalar(T4, T2, 0.75, None, OP.is_gt), inc="VE")
            op("vector", lambda v: v.tensor_tensor(T3, T3, T4, OP.add),
               waits=[("VE", V("VE"))], inc="VE")
            op("vector", lambda v: v.tensor_scalar(T4, T2, 1.25, None, OP.is_gt), inc="VE")
            op("vector", lambda v: v.tensor_tensor(T3, T3, T4, OP.add),
               waits=[("VE", V("VE"))], inc="VE")
            op("vector", lambda v: v.tensor_scalar(T4, T2, 1.75, None, OP.is_gt), inc="VE")
            op("vector", lambda v: v.tensor_tensor(T3, T3, T4, OP.add),
               waits=[("VE", V("VE"))], inc="VE")
            op("vector", lambda v: v.tensor_scalar(T4, T2, 2.5, 2.0, OP.is_gt, OP.mult),
               inc="VE")
            op("vector", lambda v: v.tensor_tensor(T3, T3, T4, OP.add),
               waits=[("VE", V("VE"))], inc="VE")
            op("vector", lambda v: v.tensor_scalar(T4, T2, 3.5, 2.0, OP.is_gt, OP.mult),
               inc="VE")
            op("vector", lambda v: v.tensor_tensor(T3, T3, T4, OP.add),
               waits=[("VE", V("VE"))], inc="VE")
            op("vector", lambda v: v.tensor_scalar(T4, T2, 5.0, 4.0, OP.is_gt, OP.mult),
               inc="VE")
            op("vector", lambda v: v.tensor_tensor(T3, T3, T4, OP.add),
               waits=[("VE", V("VE"))], inc="VE")
            ssb = bass.AP(ARF, F_ss, [[ARF_N, 128], [1, nbv], [0, 16]])
            op("vector", lambda v: v.tensor_tensor(
                T4.rearrange("p (b k) -> p b k", k=16),
                T3.rearrange("p (b k) -> p b k", k=16), ssb, OP.mult),
                waits=[("VE", V("VE"))], inc="VE")
            qd = op("vector", lambda v: v.tensor_tensor(out_ap, T4, T1, OP.mult),
                    waits=[("AC", a_sgn), ("VE", V("VE"))] + list(out_war),
                    inc="VE")
            return qd, [("AC", a_sgn), ("VE", red_v)]

        vrep_ready = [0]
        xtg = {0: [], 1: []}

        def ln_tile(b, i, src_dram, c1_off, sh_off, out_bf, sl, og=()):
            xt = fview(F_xt, 2, 1024)[:, sl, :]
            ldx = dma("sync", xt,
                      bass.AP(src_dram, (b * S + i * 128) * E, [[1024, 128], [1, 1024]]),
                      waits=list(xtg[sl]))
            bn12 = ARF[:, F_bn:F_bn + 12]
            op("vector", lambda v: v.bn_stats(bn12[:, 0:6], xt[:, 0:512]),
               waits=[("dS", ldx), ("VE", V("VE"))], inc="VE")
            op("vector", lambda v: v.bn_stats(bn12[:, 6:12], xt[:, 512:1024]), inc="VE")
            bn2v = op("vector", lambda v: v.bn_aggr(ARF[:, F_st:F_st + 2], bn12),
                      waits=[("VE", V("VE"))], inc="VE")
            st = ARF
            wv2 = [("VE", V("VE")), ("AC", V("AC"))]
            op("scalar", lambda s: s.activation(
                st[:, F_st + 2:F_st + 3], st[:, F_st + 1:F_st + 2], AF.Ln,
                bias=ARF[:, F_eps:F_eps + 1]), waits=wv2, inc="AC")
            op("scalar", lambda s: s.activation(
                st[:, F_st + 2:F_st + 3], st[:, F_st + 2:F_st + 3], AF.Exp, scale=-0.5),
                inc="AC")
            nm = op("vector", lambda v: v.tensor_scalar(
                st[:, F_st + 3:F_st + 4], st[:, F_st + 0:F_st + 1],
                st[:, F_st + 2:F_st + 3], -1.0, OP.mult, OP.mult),
                waits=[("AC", V("AC"))], inc="VE")
            xh = op("scalar", lambda s: s.activation(
                out_bf, xt, AF.Identity, scale=st[:, F_st + 2:F_st + 3],
                bias=st[:, F_st + 3:F_st + 4]),
                waits=[("VE", nm)] + list(og), inc="AC")
            xtg[sl] = [("AC", xh), ("VE", bn2v)]
            op("vector", lambda v, g=nview(c1_off, 1024): v.tensor_tensor(
                out_bf, out_bf, g, OP.mult),
                waits=[("AC", xh), ("dS", vrep_ready[0])], inc="VE")
            return op("vector", lambda v, g=nview(sh_off, 1024): v.tensor_tensor(
                out_bf, out_bf, g, OP.add),
                waits=[("VE", V("VE"))], inc="VE")


        def vro(j):
            return O_vrep + j * 1024

        hT_guard = [("TE", V("TE"))]
        hsg = {0: [], 1: []}

        def emit_lnA(b, i):
            hs = nview(O_hst, 4, 1024)[:, i % 2, :]
            hdone = ln_tile(b, i, x_d, vro(1), vro(0), hs, i % 2, og=hsg[i % 2])
            tr = dma("sync", nap(A_hT + i * 128, [[2048, 8], [1, 128]]), hs,
                     waits=[("VE", hdone)] + hT_guard, transpose=True)
            hsg[i % 2] = [("dS", tr)]

        def emit_vrep(b):
            for j in range(6):
                dma("sync", nview(vro(j), 1024),
                    bass.AP(modr_d, b * 6144 + j * 1024,
                            [[1, 1], [0, 128], [1, 1024]]),
                    waits=[("VE", V("VE")), ("dS", prep_dma_done)])
            vrep_ready[0] = V("dS")

        emit_vrep(0)
        pending_ln = list(range(NT))

        wfp = ARF[:, F_wfp:F_wfp + 2816]
        wv0, wa0 = V("VE"), V("AC")
        WFPO = {0: F_wfp, 1: F_xt}
        MTK = {0: B_mtok, 1: B_mtok + 2816}
        MQO = {0: B_mq, 1: B_dfe}
        T1O = {0: B_T1, 1: B_mid + 2048}
        sguard = {0: [("VE", wv0), ("AC", wa0)], 1: [("VE", wv0), ("AC", wa0)]}
        trg = {0: [], 1: []}
        outg = {0: [], 1: []}
        t1g = {0: [], 1: []}
        ci = 0
        for wsrc, wdst in ((wg_d, wgT_d), (wu_d, wuT_d)):
            for icp in range(0, IC, 2):
                c = ci % 2
                ci += 1
                wf = ARF[:, WFPO[c]:WFPO[c] + 2048]
                ld = None
                for k in range(2):
                    ld = dma("sync", wf[:, k * 1024:(k + 1) * 1024],
                             bass.AP(wsrc, (icp + k) * 128 * 1024,
                                     [[1024, 128], [1, 1024]]),
                             waits=(list(sguard[c]) + [("dS", prep_dma_done)])
                             if k == 0 else ())
                qd, g = quant_generic(wf, 2048, 128, nview(MTK[c], 2048),
                                      [("dS", ld)], t1o=T1O[c],
                                      t1war=t1g[c], out_war=outg[c])
                sguard[c] = g
                t1g[c] = [("VE", qd)]
                tr = dma("sync", nap(MQO[c], [[128, 16], [1, 128]]),
                         nview(MTK[c], 2048),
                         waits=[("VE", qd)] + trg[c], transpose=True)
                outg[c] = [("dS", tr)]
                st = None
                for k in range(2):
                    st = dma("sync",
                             bass.AP(wdst, (icp + k) * 128,
                                     [[I, 128], [128 * I, 8], [1, 128]]),
                             nap(MQO[c] + k * 1024, [[128, 8], [1, 128]]),
                             waits=[("dS", tr)] if k == 0 else ())
                trg[c] = [("dS", st)]
        xtg[0] = list(sguard[1])
        xtg[1] = list(sguard[1])
        for ecc in range(EC):
            c = ci % 2
            ci += 1
            ld = dma("sync", wfp[:, 0:2816],
                     bass.AP(wd_d, ecc * 128 * 2816, [[2816, 128], [1, 2816]]),
                     waits=list(sguard[0]))
            qd, g = quant_generic(wfp[:, 0:2816], 2816, 176, nview(MTK[c], 2816),
                                  [("dS", ld)], t1o=None,
                                  t1war=t1g[0] + t1g[1], out_war=outg[c])
            sguard[0] = g
            t1g[0] = [("VE", qd)]
            t1g[1] = []
            tr = dma("sync", nap(MQO[c], [[128, 22], [1, 128]]),
                     nview(MTK[c], 2816),
                     waits=[("VE", qd)] + trg[c], transpose=True)
            outg[c] = [("dS", tr)]
            st = dma("sync",
                     bass.AP(wdT_d, ecc * 128, [[E, 128], [128 * E, 22], [1, 128]]),
                     nap(MQO[c], [[128, 22], [1, 128]]), waits=[("dS", tr)])
            trg[c] = [("dS", st)]
            emit_lnA(0, pending_ln.pop(0))
            emit_lnA(0, pending_ln.pop(0))
        wq_scratch_done = V("dS")
        dfetr_g0 = [("dS", wq_scratch_done)]

        eregion_guard = [("dS", wq_scratch_done)]
        attnT_guard = [("dS", wq_scratch_done)]
        proj_te_done = [0]
        outproj_te_done = [0]
        wqkv_g = {0: [], 1: []}
        wo_g = {0: [], 1: []}
        vfe_g = {0: [], 1: []}
        attq_g = {0: [], 1: []}
        otok_g = [[]]
        wgu_g = {0: [], 1: []}
        wd_g = {0: [], 1: []}
        xq_g = [[]]
        h2_g = [[]]
        midtr_g = [[]]
        dfetr_g = [dfetr_g0]
        upmm_te = [0]
        for b in range(2):
            if b == 1:
                emit_vrep(1)

            if b == 1:
                for i in range(NT):
                    emit_lnA(1, i)
            hT = nview(A_hT, 8, 2048)
            hT_done = V("dS")

            attq = nview(A_attq, 2, 4, 128)
            Ab = fview(F_ab, 2, 4, 132)
            EOFF = (A_E0, A_E1)
            eguard = {0: None, 1: None}

            for hp in range(NV):
                wslot = hp % 2
                wq_t = nview(A_wqkv + (0 * 2 + wslot) * 1024, 8, 128)
                wk_t = nview(A_wqkv + (1 * 2 + wslot) * 1024, 8, 128)
                wv_t = nview(A_wqkv + (2 * 2 + wslot) * 1024, 8, 128)
                wG = list(wqkv_g[wslot])
                for wt, wdram in ((wq_t, wq_d), (wk_t, wk_d), (wv_t, wv_d)):
                    dma("gpsimd", wt,
                        bass.AP(wdram, hp * 128, [[E, 128], [128 * E, 8], [1, 128]]),
                        waits=wG)
                wW = [("dG", V("dG")), ("dS", hT_done)]
                QTo = [A_QK + j * 2048 for j in range(2)]
                KTo = [A_QK + (2 + j) * 2048 for j in range(2)]
                qk_ac = {}
                for w_i, (wt, outs) in enumerate(((wq_t, QTo), (wk_t, KTo))):
                    for j in range(2):
                        for t4 in range(4):
                            pskey = ("proj", w_i, t4 % 2)
                            ps = pb[t4 % 2][:, w_i * 512:w_i * 512 + 512]
                            t = None
                            for kcc in range(EC):
                                t = op("tensor", lambda t_, wt=wt, j=j, k=kcc, t4=t4, ps=ps:
                                       t_.matmul(ps[0:64, :], wt[:, k, j * 64:(j + 1) * 64],
                                                 hT[:, k, t4 * 512:(t4 + 1) * 512],
                                                 start=(k == 0), stop=(k == EC - 1)),
                                       waits=(wW + bank_wait(pskey)) if kcc == 0 else (),
                                       inc="TE" if kcc == EC - 1 else None)
                            scl = 0.125 if w_i == 0 else 1.0
                            a = op("scalar", lambda s, o=outs[j], t4=t4, ps=ps, sc=scl:
                                   s.activation(ARN[0:64, o + t4 * 512:o + (t4 + 1) * 512],
                                                ps[0:64, :], AF.Copy, scale=sc),
                                   waits=[("TE", t)], inc="AC")
                            bank_release(pskey, "AC", a)
                            qk_ac[(w_i, j)] = a
                vfe = nview(A_vfe, 2, 512)
                v_tr = None
                for t4 in range(4):
                    pskey = ("projv", t4 % 2)
                    ps = pb[2][:, (t4 % 2) * 512:(t4 % 2) * 512 + 512]
                    t = None
                    for kcc in range(EC):
                        t = op("tensor", lambda t_, k=kcc, t4=t4, ps=ps, wt=wv_t:
                               t_.matmul(ps, wt[:, k, :],
                                         hT[:, k, t4 * 512:(t4 + 1) * 512],
                                         start=(k == 0), stop=(k == EC - 1)),
                               waits=(wW + bank_wait(pskey)) if kcc == 0 else (),
                               inc="TE" if kcc == EC - 1 else None)
                    a = op("scalar", lambda s, t4=t4, ps=ps: s.activation(
                        vfe[:, t4 % 2, :], ps, AF.Copy),
                        waits=[("TE", t)] + list(vfe_g[t4 % 2]), inc="AC")
                    bank_release(pskey, "AC", a)
                    vt_guard = eguard[0] or []
                    v_tr = dma("sync",
                               nap(A_Vt + 4 * t4 * 144, [[144, 4], [1, 128]]),
                               vfe[:, t4 % 2, :],
                               waits=[("AC", a)] + vt_guard, transpose=True)
                    vfe_g[t4 % 2] = [("dS", v_tr)]
                wqkv_g[wslot] = [("TE", V("TE"))]
                ones_v = op("vector", lambda v, sl=wslot: v.memset(
                    nap(A_Vt + 128, [[144, 16], [1, 1]]), 1.0),
                    waits=[("dS", v_tr)], inc="VE")
                vtok_ready = [("dS", v_tr), ("VE", ones_v)]

                for qb_i in range(NQB):
                    q0 = qb_i * QB
                    exp_ac = {}
                    for j in range(2):
                        for g in range(8):
                            bank = pb[g % 2]
                            pskey = ("att_s", g % 2)
                            t = None
                            for u in range(2):
                                kcc = 2 * g + u
                                t = op("tensor", lambda t_, k=kcc, u=u, bank=bank, q0=q0,
                                       KT=KTo[j], QT=QTo[j]:
                                       t_.matmul(bank[:, u * 512:(u + 1) * 512],
                                                 ARN[0:64, KT + k * 128:KT + (k + 1) * 128],
                                                 ARN[0:64, QT + q0:QT + q0 + QB],
                                                 start=True, stop=True),
                                       waits=(bank_wait(pskey)
                                              + [("AC", qk_ac[(0, j)]), ("AC", qk_ac[(1, j)])]
                                              + (eguard[j] or [])) if u == 0 else (),
                                       inc="TE" if u == 1 else None)
                            a = op("scalar", lambda s, j=j, g=g, bank=bank: s.activation(
                                nap(EOFF[j] + 2 * g * 512, [[512, 2], [1, 512]]),
                                bank[:, :], AF.Exp),
                                waits=[("TE", t)]
                                + (eregion_guard if (hp == 0 and qb_i == 0 and g == 0)
                                   else []),
                                inc="AC")
                            bank_release(pskey, "AC", a)
                            exp_ac[(j, g)] = a
                    for j in range(2):
                        ewait = [("AC", exp_ac[(j, 7)])]
                        for qt in range(4):
                            pskey = ("att_a", j, qt % 2)
                            ps = pb[2 + j][:, (qt % 2) * 512:(qt % 2) * 512 + 129]
                            t = None
                            for kcc in range(KC):
                                t = op("tensor", lambda t_, j=j, k=kcc, qt=qt, ps=ps, sl=wslot:
                                       t_.matmul(ps,
                                                 nap(EOFF[j] + k * 512 + qt * 128, [[1, 128]]),
                                                 nap(A_Vt + k * 144, [[1, 129]]),
                                                 start=(k == 0), stop=(k == KC - 1)),
                                       waits=(ewait + vtok_ready + bank_wait(pskey))
                                       if kcc == 0 else (),
                                       inc="TE" if kcc == KC - 1 else None)
                            vv = op("vector", lambda v, j=j, qt=qt, ps=ps: v.tensor_copy(
                                Ab[:, j, qt, 0:129], ps), waits=[("TE", t)], inc="VE")
                            bank_release(pskey, "VE", vv)
                        eguard[j] = [("TE", V("TE"))]
                    cmb = ARF[:, F_cmb:F_cmb + 16]
                    op("vector", lambda v: v.reciprocal(
                        cmb[:, 0:4],
                        bass.AP(ARF, F_ab + 128, [[ARF_N, 128], [132, 4], [1, 1]])),
                        waits=[("VE", V("VE"))], inc="VE")
                    op("vector", lambda v: v.reciprocal(
                        cmb[:, 4:8],
                        bass.AP(ARF, F_ab + 528 + 128, [[ARF_N, 128], [132, 4], [1, 1]])),
                        inc="VE")
                    op("vector", lambda v: v.tensor_tensor(
                        cmb[:, 8:12], cmb[:, 0:4],
                        bass.AP(ARF, F_ab + 128, [[ARF_N, 128], [132, 4], [1, 1]]),
                        OP.mult), waits=[("VE", V("VE"))], inc="VE")
                    op("vector", lambda v: v.tensor_tensor(
                        cmb[:, 12:16], cmb[:, 4:8],
                        bass.AP(ARF, F_ab + 528 + 128, [[ARF_N, 128], [132, 4], [1, 1]]),
                        OP.mult), inc="VE")
                    op("vector", lambda v: v.tensor_scalar(
                        cmb[:, 8:16], cmb[:, 8:16], -1.0, 2.0, OP.mult, OP.add),
                        waits=[("VE", V("VE"))], inc="VE")
                    op("vector", lambda v: v.tensor_tensor(
                        cmb[:, 0:4], cmb[:, 0:4], cmb[:, 8:12], OP.mult),
                        waits=[("VE", V("VE"))], inc="VE")
                    op("vector", lambda v: v.tensor_tensor(
                        cmb[:, 4:8], cmb[:, 4:8], cmb[:, 12:16], OP.mult),
                        waits=[("VE", V("VE"))], inc="VE")
                    op("vector", lambda v: v.tensor_scalar(
                        cmb[:, 4:8], cmb[:, 4:8], lamrep, -1.0, OP.mult, OP.mult),
                        waits=[("VE", V("VE"))], inc="VE")
                    for qt in range(4):
                        sq = ARF[:, F_sq:F_sq + 8]
                        op("vector", lambda v, qt=qt: v.tensor_scalar(
                            Ab[:, 0, qt, 0:128], Ab[:, 0, qt, 0:128],
                            cmb[:, qt:qt + 1], None, OP.mult),
                            waits=[("VE", V("VE"))], inc="VE")
                        op("vector", lambda v, qt=qt: v.scalar_tensor_tensor(
                            Ab[:, 0, qt, 0:128], Ab[:, 1, qt, 0:128],
                            cmb[:, 4 + qt:5 + qt], Ab[:, 0, qt, 0:128], OP.mult, OP.add),
                            waits=[("VE", V("VE"))], inc="VE")
                        op("vector", lambda v, qt=qt: v.bn_stats(
                            sq[:, 0:6], Ab[:, 0, qt, 0:128]),
                            waits=[("VE", V("VE"))], inc="VE")
                        op("vector", lambda v: v.bn_aggr(sq[:, 6:8], sq[:, 0:6]),
                           waits=[("VE", V("VE"))], inc="VE")
                        op("vector", lambda v: v.tensor_scalar(
                            sq[:, 0:1], sq[:, 6:7], sq[:, 6:7], None, OP.mult),
                            waits=[("VE", V("VE"))], inc="VE")
                        op("vector", lambda v: v.tensor_tensor(
                            sq[:, 0:1], sq[:, 0:1], sq[:, 7:8], OP.add),
                            waits=[("VE", V("VE"))], inc="VE")
                        op("scalar", lambda s: s.activation(
                            sq[:, 1:2], sq[:, 0:1], AF.Ln, bias=ARF[:, F_eps:F_eps + 1]),
                            waits=[("VE", V("VE"))], inc="AC")
                        op("scalar", lambda s: s.activation(
                            sq[:, 1:2], sq[:, 1:2], AF.Exp, scale=-0.5), inc="AC")
                        op("vector", lambda v, qt=qt, sl=wslot: v.tensor_scalar(
                            attq[:, sl, qt, :], Ab[:, 0, qt, 0:128], sq[:, 1:2],
                            None, OP.mult),
                            waits=[("AC", V("AC"))] + list(attq_g[wslot]), inc="VE")
                    atr = dma("sync",
                              nap(A_attnT + hp * 2048 + q0, [[128, 4], [1, 128]]),
                              ARN[:, A_attq + wslot * 512:A_attq + (wslot + 1) * 512],
                              waits=[("VE", V("VE"))] + list(attnT_guard),
                              transpose=True)
                    attq_g[wslot] = [("dS", atr)]

            attnT_done = V("dS")
            proj_te_done[0] = V("TE")
            ofeat = nview(A_E0, 8, 2048)
            for nt in range(EC):
                wsl = nt % 2
                wo_t = nview(A_wo + wsl * 1024, 8, 128)
                dma("gpsimd", wo_t,
                    bass.AP(wo_d, nt * 128, [[E, 128], [128 * E, 8], [1, 128]]),
                    waits=list(wo_g[wsl]))
                wo_flat = ARN[:, A_wo + wsl * 1024:A_wo + (wsl + 1) * 1024]
                sc = op("vector", lambda v, w=wo_flat: v.tensor_scalar(
                    w, w, ARF[:, F_wf:F_wf + 1], None, OP.mult),
                    waits=[("dG", V("dG"))], inc="VE")
                wo_mm_last = None
                for t4 in range(4):
                    pskey = ("oproj", t4 % 2)
                    ps = pb[t4 % 2][:, 0:512]
                    t = None
                    for kcc in range(EC):
                        t = op("tensor", lambda t_, k=kcc, t4=t4, ps=ps, w=wo_t: t_.matmul(
                            ps, w[:, k, :],
                            ARN[:, A_attnT + k * 2048 + t4 * 512:A_attnT + k * 2048 + (t4 + 1) * 512],
                            start=(k == 0), stop=(k == EC - 1)),
                            waits=([("VE", sc), ("dS", attnT_done)] + bank_wait(pskey))
                            if kcc == 0 else (),
                            inc="TE" if kcc == EC - 1 else None)
                    a = op("scalar", lambda s, nt=nt, t4=t4, ps=ps: s.activation(
                        ofeat[:, nt, t4 * 512:(t4 + 1) * 512], ps, AF.Copy),
                        waits=[("TE", t)], inc="AC")
                    bank_release(pskey, "AC", a)
                wo_g[wsl] = [("TE", V("TE")), ("VE", sc)]
            ofeat_ac = V("AC")
            outproj_te_done[0] = V("TE")
            for t4 in range(4):
                tr = None
                og4 = list(otok_g[0]) + (hsg[0] + hsg[1] if t4 == 0 else [])
                for nt in range(EC):
                    tr = dma("sync",
                             nap(O_hst + nt * 128, [[1024, 4], [1, 128]]),
                             ofeat[:, nt, t4 * 512:(t4 + 1) * 512],
                             waits=[("AC", ofeat_ac)] + og4,
                             transpose=True)
                vvs = []
                for ii in range(4):
                    i = t4 * 4 + ii
                    otok = nview(O_hst, 4, 1024)[:, ii, :]
                    xt = fview(F_xt, 2, 1024)[:, ii % 2, :]
                    ldx = dma("sync", xt,
                              bass.AP(x_d, (b * S + i * 128) * E, [[1024, 128], [1, 1024]]),
                              waits=list(xtg[ii % 2]))
                    op("vector", lambda v, o=otok, g=nview(vro(2), 1024): v.tensor_tensor(
                        o, o, g, OP.mult),
                        waits=[("dS", tr)], inc="VE")
                    vv = op("vector", lambda v, o=otok, x=xt: v.tensor_tensor(
                        x, x, o, OP.add), waits=[("dS", ldx), ("VE", V("VE"))], inc="VE")
                    vvs.append(vv)
                    st_d = dma("sync",
                               bass.AP(x1_d, (b * S + i * 128) * E, [[1024, 128], [1, 1024]]),
                               xt, waits=[("VE", vv)])
                    xtg[ii % 2] = [("dS", st_d)]
                otok_g[0] = [("VE", vvs[-1])]
            x1_done = V("dS")
            eregion_guard = [("dS", x1_done)]

            xqT = nview(B_xqT, 8, 512)
            sg = nview(B_sgmqT, 22, 512)
            mid = nview(B_mid, 22, 512)
            def emit_stage1(tb):
                for ii in range(4):
                    i = tb * 4 + ii
                    h2 = nview(B_h2, 1024)
                    hdone = ln_tile(b, i, x1_d, vro(4), vro(3), h2, ii % 2,
                                    og=h2_g[0])
                    xqv = nview(B_xq, 1024)
                    qd, qsrc = quant_generic(h2, 1024, 64, xqv,
                                             [("VE", hdone)] + list(xq_g[0]))
                    h2_g[0] = qsrc
                    xtr = dma("sync", nap(B_xqT + ii * 128, [[512, 8], [1, 128]]), xqv,
                              waits=[("VE", qd), ("TE", upmm_te[0])], transpose=True)
                    xq_g[0] = [("dS", xtr)]
                return V("dS")

            xq_done = emit_stage1(0)
            mtok4_g = [[]]
            stg3_t1g = [[]]
            mqsl_g = {0: [], 1: list(dfetr_g[0])}
            for tb in range(NFTB):
                sg_ac = {}
                mid_ve = {}
                groups = [list(range(g0, min(IC, g0 + 4))) for g0 in range(0, IC, 4)]
                for phase in range(2):
                    for gi, nts in enumerate(groups):
                        slot = gi % 2
                        lw = None
                        for ni, nt in enumerate(nts):
                            lw = dma("sync", nview(B_wgu + slot * 4096 + ni * 1024, 8, 128),
                                     bass.AP(wgT_d if phase == 0 else wuT_d, nt * 128,
                                             [[I, 128], [128 * I, 8], [1, 128]]),
                                     waits=list(wgu_g[slot]) if ni == 0 else ())
                        for ni, nt in enumerate(nts):
                            w_t = nview(B_wgu + slot * 4096 + ni * 1024, 8, 128)
                            pskey = ("ffn_gu", phase, nt % 2)
                            ps = pb[1 + phase][:, (nt % 2) * 512:(nt % 2) * 512 + 512]
                            t = None
                            for kcc in range(EC):
                                t = op("tensor", lambda t_, k=kcc, ps=ps, w=w_t: t_.matmul(
                                    ps, w[:, k, :], xqT[:, k, :],
                                    start=(k == 0), stop=(k == EC - 1)),
                                    waits=([("dS", lw), ("dS", xq_done)]
                                           + bank_wait(pskey)) if kcc == 0 else (),
                                    inc="TE" if kcc == EC - 1 else None)
                            sgf = ARN[:, B_sgmqT + nt * 512:B_sgmqT + nt * 512 + 512]
                            mdf = ARN[:, B_mid + nt * 512:B_mid + nt * 512 + 512]
                            if phase == 0:
                                a = op("scalar", lambda s, o=sgf, ps=ps: s.activation(
                                    o, ps, AF.Silu), waits=[("TE", t)], inc="AC")
                                bank_release(pskey, "AC", a)
                                sg_ac[nt] = a
                            else:
                                vv = op("vector", lambda v, o=mdf, g2=sgf, ps=ps:
                                        v.tensor_tensor(o, g2, ps, OP.mult),
                                        waits=[("TE", t), ("AC", sg_ac[nt])]
                                        + list(midtr_g[0]), inc="VE")
                                bank_release(pskey, "VE", vv)
                                mid_ve[nt] = vv
                        wgu_g[slot] = [("TE", V("TE"))]
                upmm_te[0] = V("TE")
                mid_all = V("VE")
                mqT_tr = []
                trs = []
                for nt in range(IC):
                    trs.append(dma(
                        "sync",
                        bass.AP(ARN, B_T1 + nt * 128,
                                [[ARN_N, 128], [2816, 4], [1, 128]]),
                        ARN[:, B_mid + nt * 512:B_mid + nt * 512 + 512],
                        waits=[("VE", mid_ve[nt])]
                        + (mtok4_g[0] if nt == 0 else []), transpose=True))
                for tile in range(4):
                    sl = tile % 2
                    mqo = B_mq if sl == 0 else B_dfe
                    mtk = ARN[:, B_T1 + tile * 2816:B_T1 + (tile + 1) * 2816]
                    qd, qsrc = quant_generic(mtk, 2816, 176, nview(mqo, 2816),
                                             [("dS", trs[-1])],
                                             t1o=B_mtok,
                                             t1war=stg3_t1g[0],
                                             out_war=mqsl_g[sl],
                                             tmpo=(B_mtok + 2816, B_wgu,
                                                   B_wgu + 4096))
                    stg3_t1g[0] = [("VE", qd)]
                    if tile == 0:
                        mtok4_g[0] = []
                    mtok4_g[0] = mtok4_g[0] + qsrc
                    mqT_tr.append(dma(
                        "sync", nap(B_sgmqT + tile * 128, [[512, 22], [1, 128]]),
                        nview(mqo, 2816), waits=[("VE", qd)], transpose=True))
                    mqsl_g[sl] = [("dS", mqT_tr[-1])]
                for _sl in (0, 1):
                    wgu_g[_sl] = wgu_g[_sl] + [("VE", stg3_t1g[0][0][1])]
                if tb < NFTB - 1:
                    xq_done = emit_stage1(tb + 1)
                dfe_ac = V("AC")
                for ntE in range(EC):
                    slot = ntE % 2
                    wd_t = nview(B_wd2 if slot else B_wd, 22, 128)
                    lw = dma("sync", wd_t,
                             bass.AP(wdT_d, ntE * 128, [[E, 128], [128 * E, 22], [1, 128]]),
                             waits=list(wd_g[slot]) if wd_g[slot]
                             else [("TE", V("TE"))])
                    pskey = ("ffn_d", ntE % 2)
                    ps = pb[3][:, (ntE % 2) * 512:(ntE % 2) * 512 + 512]
                    t = None
                    for kcc in range(IC):
                        t = op("tensor", lambda t_, k=kcc, ps=ps, w=wd_t: t_.matmul(
                            ps, w[:, k, :],
                            nap(B_sgmqT + k * 512, [[1, 512]]),
                            start=(k == 0), stop=(k == IC - 1)),
                            waits=([("dS", lw), ("dS", mqT_tr[-1])] + bank_wait(pskey))
                            if kcc == 0 else (),
                            inc="TE" if kcc == IC - 1 else None)
                    wd_g[slot] = [("TE", t)]
                    a = op("scalar", lambda s, ntE=ntE, ps=ps: s.activation(
                        nview(B_dfe, 8, 512)[:, ntE, :], ps, AF.Copy),
                        waits=[("TE", t)] + list(dfetr_g[0]), inc="AC")
                    bank_release(pskey, "AC", a)
                dfe_done = V("AC")
                tr = None
                for nt in range(EC):
                    tr = dma("sync",
                             nap(O_hst + nt * 128, [[1024, 4], [1, 128]]),
                             nview(B_dfe, 8, 512)[:, nt, :],
                             waits=[("AC", dfe_done)] + list(otok_g[0]),
                             transpose=True)
                dfetr_g[0] = [("dS", tr)]
                mqsl_g[1] = mqsl_g[1] + [("dS", tr)]
                vvs = []
                for ii in range(4):
                    i = tb * 4 + ii
                    dtok = nview(O_hst, 4, 1024)[:, ii, :]
                    xt = fview(F_xt, 2, 1024)[:, ii % 2, :]
                    ldx = dma("sync", xt,
                              bass.AP(x1_d, (b * S + i * 128) * E, [[1024, 128], [1, 1024]]),
                              waits=list(xtg[ii % 2]))
                    op("vector", lambda v, o=dtok, g=nview(vro(5), 1024): v.tensor_tensor(
                        o, o, g, OP.mult),
                        waits=[("dS", tr)], inc="VE")
                    vv = op("vector", lambda v, o=dtok, x=xt: v.tensor_tensor(
                        x, x, o, OP.add), waits=[("dS", ldx), ("VE", V("VE"))], inc="VE")
                    vvs.append(vv)
                    st_d = dma("sync",
                               bass.AP(y_d, (b * S + i * 128) * E, [[1024, 128], [1, 1024]]),
                               xt, waits=[("VE", vv)])
                    xtg[ii % 2] = [("dS", st_d)]
                otok_g[0] = [("VE", vvs[-1])]
            batch_guard = [(x, V(x)) for x in ("TE", "AC", "VE", "dS", "dG", "PL")]
            hT_guard = list(batch_guard)
            attnT_guard = list(batch_guard)
            eregion_guard = list(batch_guard)
            for _sl in (0, 1):
                wqkv_g[_sl] = list(batch_guard)
                wo_g[_sl] = list(batch_guard)
                vfe_g[_sl] = list(batch_guard)
                attq_g[_sl] = list(batch_guard)
                wd_g[_sl] = list(batch_guard)

        op("sync", lambda e: e.nop(),
           waits=[(s, V(s)) for s in SEM_NAMES])

        with nc.Block() as block:
            def replay(name):
                def emit(eng):
                    last = {}
                    for fn, waits, inc, k, cum in em.ev[name]:
                        for (s, v) in waits:
                            if last.get(s, 0) < v:
                                eng.wait_ge(SEM[s], v)
                                last[s] = v
                        inst = fn(eng)
                        if inc is not None:
                            inst.then_inc(SEM[inc], k)
                            if inc in em.chk and cum in em.chk[inc] \
                                    and last.get(inc, 0) < cum:
                                eng.wait_ge(SEM[inc], cum)
                                last[inc] = cum
                return emit

            block.sync(replay("sync"))
            block.tensor(replay("tensor"))
            block.vector(replay("vector"))
            block.scalar(replay("scalar"))
            block.gpsimd(replay("gpsimd"))

    n_inst = sum(len(v) for v in em.ev.values())
    return nc, n_inst


_CACHE = {}


def _get_nc():
    if "nc" not in _CACHE:
        _CACHE["nc"] = build_program()[0]
    return _CACHE["nc"]


def kernel(**inputs):
    def f32c(a):
        return np.ascontiguousarray(np.asarray(a, dtype=np.float32))

    x = f32c(inputs["L0_attn_query"])
    cv = f32c(inputs["L0_c"])
    shared = {
        "wq": f32c(inputs["L0_attn_Wq"]),
        "wk": f32c(inputs["L0_attn_Wk"]),
        "wv": f32c(inputs["L0_attn_Wv"]),
        "wo": f32c(inputs["L0_attn_Wo"]),
        "wg": f32c(inputs["L0_ffn_w_gate"]),
        "wu": f32c(inputs["L0_ffn_w_up"]),
        "wd": f32c(inputs["L0_ffn_w_down"]),
        "wmod": f32c(inputs["L0_wmod"]),
        "lq1": f32c(inputs["L0_attn_lq1"]).reshape(1, D),
        "lk1": f32c(inputs["L0_attn_lk1"]).reshape(1, D),
        "lq2": f32c(inputs["L0_attn_lq2"]).reshape(1, D),
        "lk2": f32c(inputs["L0_attn_lk2"]).reshape(1, D),
        "subw": f32c(inputs["L0_attn_subln_w"]).reshape(DV, 1),
    }
    in_maps = []
    for c in range(8):
        m = dict(shared)
        m["x"] = np.ascontiguousarray(x[2 * c:2 * c + 2])
        m["cv"] = np.ascontiguousarray(cv[2 * c:2 * c + 2])
        in_maps.append(m)
    nc = _get_nc()
    res = run_bass_kernel_spmd(nc, in_maps, core_ids=list(range(8)))
    out = np.concatenate([np.asarray(res.results[c]["y"], dtype=np.float32)
                          for c in range(8)], axis=0)
    return out.reshape(16, S, E)


if __name__ == "__main__":
    nc, n = build_program()
    print("built ok, instructions:", n)


# version 12
# speedup vs baseline: 1.0595x; 1.0595x over previous
import math
from contextlib import ExitStack

import numpy as np

import concourse.bass as bass
import concourse.mybir as mybir
from concourse.bass_utils import run_bass_kernel_spmd

F32 = mybir.dt.float32
BF16 = mybir.dt.bfloat16
U8 = mybir.dt.uint8
AF = mybir.ActivationFunctionType
OP = mybir.AluOpType

LAMBDA_INIT = 0.8 - 0.6 * math.exp(-0.3 * 12)
EPS = 1e-5

S, E, I, D, DV = 2048, 1024, 2816, 64, 128
NT, EC, IC = 16, 8, 22
NV, QB, NQB, KC = 8, 512, 4, 16
NFTB = 4
ARN_N = 84000
ARF_N = 8400
ENGINES = ("sync", "tensor", "vector", "scalar", "gpsimd")
SEM_NAMES = ("dS", "dG", "dA", "TE", "AC", "VE", "PL")


class EM:

    def __init__(self):
        self.ev = {e: [] for e in ENGINES}
        self.cnt = {s: 0 for s in SEM_NAMES}
        self.chk = {"dS": set(), "dG": set(), "dA": set()}

    ENG_SEM = {"vector": "VE", "scalar": "AC", "gpsimd": "PL"}

    def op(self, eng, fn, waits=(), inc=None, k=1):
        ws = [(s, v) for (s, v) in waits if v > 0]
        es = self.ENG_SEM.get(eng)
        if es is not None:
            if inc is None:
                inc = es
            if self.cnt[es] > 0:
                ws.append((es, self.cnt[es]))
        for (sm, v) in ws:
            if sm in self.chk:
                self.chk[sm].add(v)
        if inc is not None:
            self.cnt[inc] += k
        cum = self.cnt[inc] if inc else None
        self.ev[eng].append((fn, tuple(ws), inc, k, cum))
        return cum

    def val(self, s):
        return self.cnt[s]


def build_program():
    nc = bass.Bass("TRN2", target_bir_lowering=False)

    x_d = nc.dram_tensor("x", [2, S, E], F32, kind="ExternalInput")
    wq_d = nc.dram_tensor("wq", [E, E], F32, kind="ExternalInput")
    wk_d = nc.dram_tensor("wk", [E, E], F32, kind="ExternalInput")
    wv_d = nc.dram_tensor("wv", [E, E], F32, kind="ExternalInput")
    wo_d = nc.dram_tensor("wo", [E, E], F32, kind="ExternalInput")
    wg_d = nc.dram_tensor("wg", [I, E], F32, kind="ExternalInput")
    wu_d = nc.dram_tensor("wu", [I, E], F32, kind="ExternalInput")
    wd_d = nc.dram_tensor("wd", [E, I], F32, kind="ExternalInput")
    wmod_d = nc.dram_tensor("wmod", [E, 6 * E], F32, kind="ExternalInput")
    cv_d = nc.dram_tensor("cv", [2, E], F32, kind="ExternalInput")
    lq1_d = nc.dram_tensor("lq1", [1, D], F32, kind="ExternalInput")
    lk1_d = nc.dram_tensor("lk1", [1, D], F32, kind="ExternalInput")
    lq2_d = nc.dram_tensor("lq2", [1, D], F32, kind="ExternalInput")
    lk2_d = nc.dram_tensor("lk2", [1, D], F32, kind="ExternalInput")
    subw_d = nc.dram_tensor("subw", [DV, 1], F32, kind="ExternalInput")
    y_d = nc.dram_tensor("y", [2, S, E], F32, kind="ExternalOutput")
    x1_d = nc.dram_tensor("x1s", [2, S, E], F32)
    wgT_d = nc.dram_tensor("wgTs", [E, I], BF16)
    wuT_d = nc.dram_tensor("wuTs", [E, I], BF16)
    wdT_d = nc.dram_tensor("wdTs", [I, E], BF16)
    modr_d = nc.dram_tensor("modrs", [1, 2 * 6144], BF16)

    em = EM()
    op = em.op
    V = em.val

    ctx = ExitStack()
    with ctx:
        en = ctx.enter_context
        ARN = en(nc.sbuf_tensor("ARN", [128, ARN_N], BF16))
        ARF = en(nc.sbuf_tensor("ARF", [128, ARF_N], F32))
        ARU = en(nc.sbuf_tensor("ARU", [128, 1024], U8))
        BCS = en(nc.sbuf_tensor("BCS", [128, 16], F32))
        pb = [en(nc.psum_tensor(f"pb{i}", [128, 1024], F32)) for i in range(4)]
        SEM = {s: en(nc.semaphore(s)) for s in SEM_NAMES}

        cur = [0]

        def nalloc(n):
            o = cur[0]
            cur[0] += n
            assert cur[0] <= ARN_N, f"ARN overflow {cur[0]}"
            return o

        O_vrep = nalloc(6 * 1024)
        O_hst = nalloc(4 * 1024)
        O_sml = nalloc(1664)
        OV = cur[0]
        A_hT = OV
        A_E0 = A_hT + 8 * 2048
        A_E1 = A_E0 + 16 * 512
        A_attnT = A_E1 + 16 * 512
        A_QK = A_attnT + 8 * 2048
        A_Vt = A_QK + 4 * 2048
        A_attq = A_Vt + 16 * 144
        A_vfe = A_attq + 2 * 4 * 128
        A_wqkv = A_vfe + 2 * 512
        A_wo = A_wqkv + 3 * 2 * 1024
        A_end = A_wo + 2 * 1024
        assert A_end <= ARN_N, A_end
        O_modr = A_E0
        B_xqT = OV
        B_sgmqT = B_xqT + 8 * 512
        B_mid = B_sgmqT + 22 * 512
        B_mtok = B_mid + 22 * 512
        B_T1 = B_mtok + 2 * 2816
        B_T2 = B_T1 + 2816
        B_T3 = B_T2 + 2816
        B_T4 = B_T3 + 2816
        B_mq = B_T4 + 2816
        B_dfe = B_mq + 2816
        B_h2 = B_dfe + 8 * 512
        B_xq = B_h2 + 1024
        B_wgu = B_xq + 1024
        B_wd = B_wgu + 2 * 4 * 1024
        B_wd2 = B_wd + 22 * 128
        B_end = B_wd2 + 22 * 128
        assert B_end <= ARN_N, B_end

        fc = [0]

        def falloc(n):
            o = fc[0]
            fc[0] += n
            assert fc[0] <= ARF_N, f"ARF overflow {fc[0]}"
            return o

        F_xt = falloc(2 * 1024)
        F_ab = falloc(2 * 4 * 132)
        F_bn = falloc(16)
        F_st = falloc(8)
        F_lam = falloc(8)
        F_wf = falloc(1)
        F_eps = falloc(1)
        F_one = falloc(1)
        F_bm = falloc(176)
        F_ss = falloc(176)
        F_rc = falloc(176)
        F_cmb = falloc(16)
        F_sq = falloc(8)
        F_wfp = falloc(2816)

        def nview(off, *shape):
            tot = int(np.prod(shape))
            ap = ARN[:, off:off + tot]
            if len(shape) > 1:
                names = " ".join(f"d{i}" for i in range(len(shape)))
                ap = ap.rearrange(f"p ({names}) -> p {names}",
                                  **{f"d{i}": s for i, s in enumerate(shape)})
            return ap

        def fview(off, *shape):
            tot = int(np.prod(shape))
            ap = ARF[:, off:off + tot]
            if len(shape) > 1:
                names = " ".join(f"d{i}" for i in range(len(shape)))
                ap = ap.rearrange(f"p ({names}) -> p {names}",
                                  **{f"d{i}": s for i, s in enumerate(shape)})
            return ap

        def nap(off, aplist):
            return bass.AP(ARN, off, [[ARN_N, 128]] + aplist)

        def dma(eng, out_ap, in_ap, waits=(), transpose=False, slow=False):
            sem = {"gpsimd": "dG", "scalar": "dA"}.get(eng, "dS")
            if transpose:
                fn = lambda e, o=out_ap, i=in_ap: e.dma_start(o, i, transpose=True)
            elif slow:
                fn = lambda e, o=out_ap, i=in_ap: e.dma_start(
                    o, i, allow_slow_non_contiguous=True)
            else:
                fn = lambda e, o=out_ap, i=in_ap: e.dma_start(o, i)
            return op(eng, fn, waits, sem, 16)

        bankg = {}

        def bank_wait(key):
            return [bankg[key]] if key in bankg else []

        def bank_release(key, sem, val):
            bankg[key] = (sem, val)

        op("vector", lambda v: v.memset(ARF[:, F_eps:F_eps + 1], EPS), inc="VE")
        op("vector", lambda v: v.memset(ARF[:, F_one:F_one + 1], 1.0), inc="VE")
        dma("sync", ARF[:, F_wf:F_wf + 1], subw_d[:, :])
        op("vector", lambda v: v.tensor_scalar(
            ARF[:, F_wf:F_wf + 1], ARF[:, F_wf:F_wf + 1], 1.0 - LAMBDA_INIT, None, OP.mult),
            waits=[("dS", V("dS"))], inc="VE")

        lo = F_wfp
        dma("sync", ARF[0:1, lo:lo + 64], lq1_d[:, :])
        dma("sync", ARF[0:1, lo + 64:lo + 128], lk1_d[:, :])
        dma("sync", ARF[0:1, lo + 128:lo + 192], lq2_d[:, :])
        dma("sync", ARF[0:1, lo + 192:lo + 256], lk2_d[:, :])
        op("vector", lambda v: v.tensor_tensor(
            ARF[0:1, lo + 256:lo + 320], ARF[0:1, lo:lo + 64],
            ARF[0:1, lo + 64:lo + 128], OP.mult), waits=[("dS", V("dS"))], inc="VE")
        op("vector", lambda v: v.tensor_tensor(
            ARF[0:1, lo + 320:lo + 384], ARF[0:1, lo + 128:lo + 192],
            ARF[0:1, lo + 192:lo + 256], OP.mult), inc="VE")
        op("vector", lambda v: v.reduce_sum(
            ARF[0:1, F_lam:F_lam + 1], ARF[0:1, lo + 256:lo + 320],
            axis=mybir.AxisListType.X), waits=[("VE", V("VE"))], inc="VE")
        op("vector", lambda v: v.reduce_sum(
            ARF[0:1, F_lam + 1:F_lam + 2], ARF[0:1, lo + 320:lo + 384],
            axis=mybir.AxisListType.X), inc="VE")
        op("scalar", lambda s: s.activation(
            ARF[0:1, F_lam + 2:F_lam + 3], ARF[0:1, F_lam:F_lam + 1], AF.Exp),
            waits=[("VE", V("VE"))], inc="AC")
        op("scalar", lambda s: s.activation(
            ARF[0:1, F_lam + 3:F_lam + 4], ARF[0:1, F_lam + 1:F_lam + 2], AF.Exp), inc="AC")
        op("vector", lambda v: v.tensor_tensor(
            ARF[0:1, F_lam + 4:F_lam + 5], ARF[0:1, F_lam + 2:F_lam + 3],
            ARF[0:1, F_lam + 3:F_lam + 4], OP.subtract),
            waits=[("AC", V("AC"))], inc="VE")
        op("vector", lambda v: v.tensor_scalar(
            ARF[0:1, F_lam + 4:F_lam + 5], ARF[0:1, F_lam + 4:F_lam + 5], 1.0, LAMBDA_INIT,
            OP.mult, OP.add), waits=[("VE", V("VE"))], inc="VE")
        op("vector", lambda v: v.memset(BCS[:, 2:3], 1.0 / 12.0), inc="VE")
        op("vector", lambda v: v.tensor_scalar(
            BCS[:, 3:4], BCS[:, 2:3], 12.0, None, OP.mult),
            waits=[("VE", V("VE"))], inc="VE")
        op("vector", lambda v: v.tensor_scalar(
            BCS[:, 3:4], BCS[:, 3:4], -1.0, 2.0, OP.mult, OP.add), inc="VE")
        op("vector", lambda v: v.tensor_tensor(
            BCS[:, 2:3], BCS[:, 2:3], BCS[:, 3:4], OP.mult),
            waits=[("VE", V("VE"))], inc="VE")
        op("vector", lambda v: v.tensor_copy(BCS[0:1, 0:1], ARF[0:1, F_lam + 4:F_lam + 5]),
           waits=[("VE", V("VE"))], inc="VE")
        dma("sync", BCS[:, 1:2],
            bass.AP(BCS, 0, [[1, 1], [0, 128], [1, 1]]), waits=[("VE", V("VE"))])
        lamrep = BCS[:, 1:2]

        csil = nview(O_sml + 128, 8, 2)
        O_cT = O_sml + 144
        O_cb = O_sml + 272
        op("vector", lambda v: v.memset(ARN[0:16, O_cb:O_cb + 1024], 0.0), inc="VE")
        dma("gpsimd", ARN[0:2, O_cb:O_cb + 1024], cv_d[:, :], waits=[("VE", V("VE"))])
        dma("sync", bass.AP(ARN, O_cT, [[ARN_N, 128], [16, 8], [1, 16]]),
            ARN[0:16, O_cb:O_cb + 1024],
            waits=[("dG", V("dG")), ("VE", V("VE"))], transpose=True)
        csil_ac = op("scalar", lambda s: s.activation(
            csil, bass.AP(ARN, O_cT, [[ARN_N, 128], [16, 8], [1, 2]]), AF.Silu),
            waits=[("dS", V("dS"))], inc="AC")

        mslot_guard = {}
        for ns in range(12):
            sl = ns % 2
            boffs = A_hT + sl * 4096
            w = [mslot_guard[sl]] if sl in mslot_guard else []
            for kcc in range(EC):
                dma("gpsimd", nview(boffs + kcc * 512, 512),
                    bass.AP(wmod_d, kcc * 128 * 6144 + ns * 512, [[6144, 128], [1, 512]]),
                    waits=w if kcc == 0 else ())
            wdg = V("dG")
            for b in range(2):
                pskey = ("pb0", b)
                ps = pb[0][0:1, b * 512:b * 512 + 512]
                t = None
                for kcc in range(EC):
                    t = op("tensor", lambda t_, b=b, k=kcc, ps=ps, bo=boffs: t_.matmul(
                        ps, csil[:, k, b:b + 1], nview(bo + k * 512, 512),
                        start=(k == 0), stop=(k == EC - 1)),
                        waits=([("dG", wdg), ("AC", csil_ac)] + bank_wait(pskey))
                        if kcc == 0 else (),
                        inc="TE" if kcc == EC - 1 else None)
                vv = op("vector", lambda v, b=b, ns=ns, ps=ps: v.tensor_copy(
                    ARN[0:1, O_modr + b * 6144 + ns * 512:O_modr + b * 6144 + (ns + 1) * 512],
                    ps), waits=[("TE", t)], inc="VE")
                bank_release(pskey, "VE", vv)
            mslot_guard[sl] = ("TE", V("TE"))
        for b in range(2):
            base = O_modr + b * 6144
            for j in (1, 4):
                op("vector", lambda v, ba=base, j=j: v.tensor_scalar(
                    ARN[0:1, ba + j * 1024:ba + (j + 1) * 1024],
                    ARN[0:1, ba + j * 1024:ba + (j + 1) * 1024], 1.0, 1.0, OP.mult, OP.add),
                    waits=[("VE", V("VE"))], inc="VE")
            for j in (2, 5):
                mg = ARN[0:1, base + j * 1024:base + (j + 1) * 1024]
                t1 = ARN[0:1, B_T1:B_T1 + 1024]
                t2 = ARN[0:1, B_T2:B_T2 + 1024]
                op("vector", lambda v, m=mg, a=t1: v.tensor_scalar(
                    a, m, 0.0, 2.0, OP.is_ge, OP.mult),
                    waits=[("VE", V("VE")), ("AC", V("AC"))], inc="VE")
                op("vector", lambda v, a=t1: v.tensor_scalar(
                    a, a, -1.0, None, OP.add), waits=[("VE", V("VE"))], inc="VE")
                op("scalar", lambda s, m=mg, b2=t2: s.activation(b2, m, AF.Abs),
                   waits=[("VE", V("VE"))], inc="AC")
                op("vector", lambda v, b2=t2: v.tensor_scalar(
                    b2, b2, 0.25, None, OP.max),
                    waits=[("AC", V("AC"))], inc="VE")
                op("vector", lambda v, m=mg, a=t1, b2=t2: v.tensor_tensor(
                    m, a, b2, OP.mult), waits=[("VE", V("VE"))], inc="VE")
        dma("sync", modr_d[:, :], ARN[0:1, O_modr:O_modr + 2 * 6144],
            waits=[("VE", V("VE"))])
        prep_dma_done = V("dS")

        def quant_generic(src, width, nb, out_ap, pre_waits, t1o=None,
                          t1war=None, out_war=(), tmpo=None):
            nbv = nb
            t2o, t3o, t4o = tmpo if tmpo is not None else (B_T2, B_T3, B_T4)
            T1 = nview(t1o, width) if t1o is not None \
                else nview(B_T1, 2816)[:, 0:width]
            T2 = nview(t2o, width)
            T3 = nview(t3o, width)
            T4 = nview(t4o, width)
            srcb = src.rearrange("p (b k) -> p b k", k=16)
            red_v = op("vector", lambda v: v.reduce_max(
                ARF[:, F_bm:F_bm + nbv], srcb, axis=mybir.AxisListType.X,
                apply_absolute_value=True),
                waits=list(pre_waits) + [("VE", V("VE"))], inc="VE")
            op("vector", lambda v: v.tensor_scalar(
                ARF[:, F_bm:F_bm + nbv], ARF[:, F_bm:F_bm + nbv], 6e-12, None,
                OP.max), inc="VE")
            op("vector", lambda v: v.reciprocal(
                ARF[:, F_rc:F_rc + nbv], ARF[:, F_bm:F_bm + nbv]),
                waits=[("VE", V("VE"))], inc="VE")
            op("vector", lambda v: v.tensor_tensor(
                ARF[:, F_ss:F_ss + nbv], ARF[:, F_bm:F_bm + nbv],
                ARF[:, F_rc:F_rc + nbv], OP.mult), waits=[("VE", V("VE"))], inc="VE")
            op("vector", lambda v: v.tensor_scalar(
                ARF[:, F_ss:F_ss + nbv], ARF[:, F_ss:F_ss + nbv], -1.0, 2.0,
                OP.mult, OP.add), inc="VE")
            op("vector", lambda v: v.tensor_tensor(
                ARF[:, F_rc:F_rc + nbv], ARF[:, F_rc:F_rc + nbv],
                ARF[:, F_ss:F_ss + nbv], OP.mult), waits=[("VE", V("VE"))], inc="VE")
            op("vector", lambda v: v.tensor_scalar(
                ARF[:, F_rc:F_rc + nbv], ARF[:, F_rc:F_rc + nbv], 6.0, None,
                OP.mult), inc="VE")
            op("vector", lambda v: v.tensor_scalar(
                ARF[:, F_ss:F_ss + nbv], ARF[:, F_bm:F_bm + nbv], BCS[:, 2:3], None,
                OP.mult), inc="VE")
            a_abs = op("scalar", lambda s: s.activation(T1, src, AF.Abs),
                       waits=list(pre_waits)
                       + (list(t1war) if t1war is not None
                          else [("VE", V("VE"))]), inc="AC")
            rcb = bass.AP(ARF, F_rc, [[ARF_N, 128], [1, nbv], [0, 16]])
            op("vector", lambda v: v.tensor_tensor(
                T2.rearrange("p (b k) -> p b k", k=16),
                T1.rearrange("p (b k) -> p b k", k=16), rcb, OP.mult),
                waits=[("AC", a_abs), ("VE", V("VE"))], inc="VE")
            a_sgn = op("scalar", lambda s: s.activation(T1, src, AF.Sign),
                       waits=[("VE", V("VE"))], inc="AC")
            op("vector", lambda v: v.tensor_scalar(T3, T2, 0.25, None, OP.is_gt),
               waits=[("VE", V("VE"))], inc="VE")
            op("vector", lambda v: v.tensor_scalar(T4, T2, 0.75, None, OP.is_gt), inc="VE")
            op("vector", lambda v: v.tensor_tensor(T3, T3, T4, OP.add),
               waits=[("VE", V("VE"))], inc="VE")
            op("vector", lambda v: v.tensor_scalar(T4, T2, 1.25, None, OP.is_gt), inc="VE")
            op("vector", lambda v: v.tensor_tensor(T3, T3, T4, OP.add),
               waits=[("VE", V("VE"))], inc="VE")
            op("vector", lambda v: v.tensor_scalar(T4, T2, 1.75, None, OP.is_gt), inc="VE")
            op("vector", lambda v: v.tensor_tensor(T3, T3, T4, OP.add),
               waits=[("VE", V("VE"))], inc="VE")
            op("vector", lambda v: v.tensor_scalar(T4, T2, 2.5, 2.0, OP.is_gt, OP.mult),
               inc="VE")
            op("vector", lambda v: v.tensor_tensor(T3, T3, T4, OP.add),
               waits=[("VE", V("VE"))], inc="VE")
            op("vector", lambda v: v.tensor_scalar(T4, T2, 3.5, 2.0, OP.is_gt, OP.mult),
               inc="VE")
            op("vector", lambda v: v.tensor_tensor(T3, T3, T4, OP.add),
               waits=[("VE", V("VE"))], inc="VE")
            op("vector", lambda v: v.tensor_scalar(T4, T2, 5.0, 4.0, OP.is_gt, OP.mult),
               inc="VE")
            op("vector", lambda v: v.tensor_tensor(T3, T3, T4, OP.add),
               waits=[("VE", V("VE"))], inc="VE")
            ssb = bass.AP(ARF, F_ss, [[ARF_N, 128], [1, nbv], [0, 16]])
            op("vector", lambda v: v.tensor_tensor(
                T4.rearrange("p (b k) -> p b k", k=16),
                T3.rearrange("p (b k) -> p b k", k=16), ssb, OP.mult),
                waits=[("VE", V("VE"))], inc="VE")
            qd = op("vector", lambda v: v.tensor_tensor(out_ap, T4, T1, OP.mult),
                    waits=[("AC", a_sgn), ("VE", V("VE"))] + list(out_war),
                    inc="VE")
            return qd, [("AC", a_sgn), ("VE", red_v)]

        vrep_ready = [0]
        xtg = {0: [], 1: []}

        def ln_tile(b, i, src_dram, c1_off, sh_off, out_bf, sl, og=()):
            xt = fview(F_xt, 2, 1024)[:, sl, :]
            ldx = dma("sync", xt,
                      bass.AP(src_dram, (b * S + i * 128) * E, [[1024, 128], [1, 1024]]),
                      waits=list(xtg[sl]))
            bn12 = ARF[:, F_bn:F_bn + 12]
            op("vector", lambda v: v.bn_stats(bn12[:, 0:6], xt[:, 0:512]),
               waits=[("dS", ldx), ("VE", V("VE"))], inc="VE")
            op("vector", lambda v: v.bn_stats(bn12[:, 6:12], xt[:, 512:1024]), inc="VE")
            bn2v = op("vector", lambda v: v.bn_aggr(ARF[:, F_st:F_st + 2], bn12),
                      waits=[("VE", V("VE"))], inc="VE")
            st = ARF
            wv2 = [("VE", V("VE")), ("AC", V("AC"))]
            op("scalar", lambda s: s.activation(
                st[:, F_st + 2:F_st + 3], st[:, F_st + 1:F_st + 2], AF.Ln,
                bias=ARF[:, F_eps:F_eps + 1]), waits=wv2, inc="AC")
            op("scalar", lambda s: s.activation(
                st[:, F_st + 2:F_st + 3], st[:, F_st + 2:F_st + 3], AF.Exp, scale=-0.5),
                inc="AC")
            nm = op("vector", lambda v: v.tensor_scalar(
                st[:, F_st + 3:F_st + 4], st[:, F_st + 0:F_st + 1],
                st[:, F_st + 2:F_st + 3], -1.0, OP.mult, OP.mult),
                waits=[("AC", V("AC"))], inc="VE")
            xh = op("scalar", lambda s: s.activation(
                out_bf, xt, AF.Identity, scale=st[:, F_st + 2:F_st + 3],
                bias=st[:, F_st + 3:F_st + 4]),
                waits=[("VE", nm)] + list(og), inc="AC")
            xtg[sl] = [("AC", xh), ("VE", bn2v)]
            op("vector", lambda v, g=nview(c1_off, 1024): v.tensor_tensor(
                out_bf, out_bf, g, OP.mult),
                waits=[("AC", xh), ("dS", vrep_ready[0])], inc="VE")
            return op("vector", lambda v, g=nview(sh_off, 1024): v.tensor_tensor(
                out_bf, out_bf, g, OP.add),
                waits=[("VE", V("VE"))], inc="VE")


        def vro(j):
            return O_vrep + j * 1024

        hT_guard = [("TE", V("TE"))]
        hsg = {0: [], 1: []}

        def emit_lnA(b, i):
            hs = nview(O_hst, 4, 1024)[:, i % 2, :]
            hdone = ln_tile(b, i, x_d, vro(1), vro(0), hs, i % 2, og=hsg[i % 2])
            tr = dma("sync", nap(A_hT + i * 128, [[2048, 8], [1, 128]]), hs,
                     waits=[("VE", hdone)] + hT_guard, transpose=True)
            hsg[i % 2] = [("dS", tr)]

        def emit_vrep(b):
            for j in range(6):
                dma("sync", nview(vro(j), 1024),
                    bass.AP(modr_d, b * 6144 + j * 1024,
                            [[1, 1], [0, 128], [1, 1024]]),
                    waits=[("VE", V("VE")), ("dS", prep_dma_done)])
            vrep_ready[0] = V("dS")

        emit_vrep(0)
        pending_ln = list(range(NT))

        wfp = ARF[:, F_wfp:F_wfp + 2816]
        wv0, wa0 = V("VE"), V("AC")
        WFPO = {0: F_wfp, 1: F_xt}
        MTK = {0: B_mtok, 1: B_mtok + 2816}
        MQO = {0: B_mq, 1: B_dfe}
        T1O = {0: B_T1, 1: B_mid + 2048}
        sguard = {0: [("VE", wv0), ("AC", wa0)], 1: [("VE", wv0), ("AC", wa0)]}
        trg = {0: [], 1: []}
        outg = {0: [], 1: []}
        t1g = {0: [], 1: []}
        ci = 0
        for wsrc, wdst in ((wg_d, wgT_d), (wu_d, wuT_d)):
            for icp in range(0, IC, 2):
                c = ci % 2
                ci += 1
                wf = ARF[:, WFPO[c]:WFPO[c] + 2048]
                ld = None
                for k in range(2):
                    ld = dma("sync", wf[:, k * 1024:(k + 1) * 1024],
                             bass.AP(wsrc, (icp + k) * 128 * 1024,
                                     [[1024, 128], [1, 1024]]),
                             waits=(list(sguard[c]) + [("dS", prep_dma_done)])
                             if k == 0 else ())
                qd, g = quant_generic(wf, 2048, 128, nview(MTK[c], 2048),
                                      [("dS", ld)], t1o=T1O[c],
                                      t1war=t1g[c], out_war=outg[c])
                sguard[c] = g
                t1g[c] = [("VE", qd)]
                qe = "scalar" if c == 1 else "sync"
                qs = "dA" if c == 1 else "dS"
                tr = dma(qe, nap(MQO[c], [[128, 16], [1, 128]]),
                         nview(MTK[c], 2048),
                         waits=[("VE", qd)] + trg[c], transpose=True)
                outg[c] = [(qs, tr)]
                st = None
                for k in range(2):
                    st = dma(qe,
                             bass.AP(wdst, (icp + k) * 128,
                                     [[I, 128], [128 * I, 8], [1, 128]]),
                             nap(MQO[c] + k * 1024, [[128, 8], [1, 128]]),
                             waits=[(qs, tr)] if k == 0 else ())
                trg[c] = [(qs, st)]
        xtg[0] = list(sguard[1])
        xtg[1] = list(sguard[1])
        for ecc in range(EC):
            c = ci % 2
            ci += 1
            ld = dma("sync", wfp[:, 0:2816],
                     bass.AP(wd_d, ecc * 128 * 2816, [[2816, 128], [1, 2816]]),
                     waits=list(sguard[0]))
            qd, g = quant_generic(wfp[:, 0:2816], 2816, 176, nview(MTK[c], 2816),
                                  [("dS", ld)], t1o=None,
                                  t1war=t1g[0] + t1g[1], out_war=outg[c])
            sguard[0] = g
            t1g[0] = [("VE", qd)]
            t1g[1] = []
            qe = "scalar" if c == 1 else "sync"
            qs = "dA" if c == 1 else "dS"
            tr = dma(qe, nap(MQO[c], [[128, 22], [1, 128]]),
                     nview(MTK[c], 2816),
                     waits=[("VE", qd)] + trg[c], transpose=True)
            outg[c] = [(qs, tr)]
            st = dma(qe,
                     bass.AP(wdT_d, ecc * 128, [[E, 128], [128 * E, 22], [1, 128]]),
                     nap(MQO[c], [[128, 22], [1, 128]]), waits=[(qs, tr)])
            trg[c] = [(qs, st)]
            emit_lnA(0, pending_ln.pop(0))
            emit_lnA(0, pending_ln.pop(0))
        wq_scratch_done = V("dS")
        wq_all = [("dS", V("dS")), ("dA", V("dA"))]
        dfetr_g0 = list(wq_all)

        eregion_guard = list(wq_all)
        attnT_guard = list(wq_all)
        proj_te_done = [0]
        outproj_te_done = [0]
        wqkv_g = {0: [], 1: []}
        wo_g = {0: [], 1: []}
        vfe_g = {0: [], 1: []}
        attq_g = {0: [], 1: []}
        otok_g = [[]]
        wgu_g = {0: [], 1: []}
        wd_g = {0: [], 1: []}
        xq_g = [[]]
        h2_g = [[]]
        midtr_g = [[]]
        dfetr_g = [dfetr_g0]
        upmm_te = [0]
        for b in range(2):
            if b == 1:
                emit_vrep(1)

            if b == 1:
                for i in range(NT):
                    emit_lnA(1, i)
            hT = nview(A_hT, 8, 2048)
            hT_done = V("dS")

            attq = nview(A_attq, 2, 4, 128)
            Ab = fview(F_ab, 2, 4, 132)
            EOFF = (A_E0, A_E1)
            eguard = {0: None, 1: None}

            for hp in range(NV):
                wslot = hp % 2
                wq_t = nview(A_wqkv + (0 * 2 + wslot) * 1024, 8, 128)
                wk_t = nview(A_wqkv + (1 * 2 + wslot) * 1024, 8, 128)
                wv_t = nview(A_wqkv + (2 * 2 + wslot) * 1024, 8, 128)
                wG = list(wqkv_g[wslot])
                for wt, wdram in ((wq_t, wq_d), (wk_t, wk_d), (wv_t, wv_d)):
                    dma("gpsimd", wt,
                        bass.AP(wdram, hp * 128, [[E, 128], [128 * E, 8], [1, 128]]),
                        waits=wG)
                wW = [("dG", V("dG")), ("dS", hT_done)]
                QTo = [A_QK + j * 2048 for j in range(2)]
                KTo = [A_QK + (2 + j) * 2048 for j in range(2)]
                qk_ac = {}
                for w_i, (wt, outs) in enumerate(((wq_t, QTo), (wk_t, KTo))):
                    for j in range(2):
                        for t4 in range(4):
                            pskey = ("proj", w_i, t4 % 2)
                            ps = pb[t4 % 2][:, w_i * 512:w_i * 512 + 512]
                            t = None
                            for kcc in range(EC):
                                t = op("tensor", lambda t_, wt=wt, j=j, k=kcc, t4=t4, ps=ps:
                                       t_.matmul(ps[0:64, :], wt[:, k, j * 64:(j + 1) * 64],
                                                 hT[:, k, t4 * 512:(t4 + 1) * 512],
                                                 start=(k == 0), stop=(k == EC - 1)),
                                       waits=(wW + bank_wait(pskey)) if kcc == 0 else (),
                                       inc="TE" if kcc == EC - 1 else None)
                            scl = 0.125 if w_i == 0 else 1.0
                            a = op("scalar", lambda s, o=outs[j], t4=t4, ps=ps, sc=scl:
                                   s.activation(ARN[0:64, o + t4 * 512:o + (t4 + 1) * 512],
                                                ps[0:64, :], AF.Copy, scale=sc),
                                   waits=[("TE", t)], inc="AC")
                            bank_release(pskey, "AC", a)
                            qk_ac[(w_i, j)] = a
                vfe = nview(A_vfe, 2, 512)
                v_tr = None
                for t4 in range(4):
                    pskey = ("projv", t4 % 2)
                    ps = pb[2][:, (t4 % 2) * 512:(t4 % 2) * 512 + 512]
                    t = None
                    for kcc in range(EC):
                        t = op("tensor", lambda t_, k=kcc, t4=t4, ps=ps, wt=wv_t:
                               t_.matmul(ps, wt[:, k, :],
                                         hT[:, k, t4 * 512:(t4 + 1) * 512],
                                         start=(k == 0), stop=(k == EC - 1)),
                               waits=(wW + bank_wait(pskey)) if kcc == 0 else (),
                               inc="TE" if kcc == EC - 1 else None)
                    a = op("scalar", lambda s, t4=t4, ps=ps: s.activation(
                        vfe[:, t4 % 2, :], ps, AF.Copy),
                        waits=[("TE", t)] + list(vfe_g[t4 % 2]), inc="AC")
                    bank_release(pskey, "AC", a)
                    vt_guard = eguard[0] or []
                    v_tr = dma("sync",
                               nap(A_Vt + 4 * t4 * 144, [[144, 4], [1, 128]]),
                               vfe[:, t4 % 2, :],
                               waits=[("AC", a)] + vt_guard, transpose=True)
                    vfe_g[t4 % 2] = [("dS", v_tr)]
                wqkv_g[wslot] = [("TE", V("TE"))]
                ones_v = op("vector", lambda v, sl=wslot: v.memset(
                    nap(A_Vt + 128, [[144, 16], [1, 1]]), 1.0),
                    waits=[("dS", v_tr)], inc="VE")
                vtok_ready = [("dS", v_tr), ("VE", ones_v)]

                for qb_i in range(NQB):
                    q0 = qb_i * QB
                    exp_ac = {}
                    for j in range(2):
                        for g in range(8):
                            bank = pb[g % 2]
                            pskey = ("att_s", g % 2)
                            t = None
                            for u in range(2):
                                kcc = 2 * g + u
                                t = op("tensor", lambda t_, k=kcc, u=u, bank=bank, q0=q0,
                                       KT=KTo[j], QT=QTo[j]:
                                       t_.matmul(bank[:, u * 512:(u + 1) * 512],
                                                 ARN[0:64, KT + k * 128:KT + (k + 1) * 128],
                                                 ARN[0:64, QT + q0:QT + q0 + QB],
                                                 start=True, stop=True),
                                       waits=(bank_wait(pskey)
                                              + [("AC", qk_ac[(0, j)]), ("AC", qk_ac[(1, j)])]
                                              + (eguard[j] or [])) if u == 0 else (),
                                       inc="TE" if u == 1 else None)
                            a = op("scalar", lambda s, j=j, g=g, bank=bank: s.activation(
                                nap(EOFF[j] + 2 * g * 512, [[512, 2], [1, 512]]),
                                bank[:, :], AF.Exp),
                                waits=[("TE", t)]
                                + (eregion_guard if (hp == 0 and qb_i == 0 and g == 0)
                                   else []),
                                inc="AC")
                            bank_release(pskey, "AC", a)
                            exp_ac[(j, g)] = a
                    for j in range(2):
                        ewait = [("AC", exp_ac[(j, 7)])]
                        for qt in range(4):
                            pskey = ("att_a", j, qt % 2)
                            ps = pb[2 + j][:, (qt % 2) * 512:(qt % 2) * 512 + 129]
                            t = None
                            for kcc in range(KC):
                                t = op("tensor", lambda t_, j=j, k=kcc, qt=qt, ps=ps, sl=wslot:
                                       t_.matmul(ps,
                                                 nap(EOFF[j] + k * 512 + qt * 128, [[1, 128]]),
                                                 nap(A_Vt + k * 144, [[1, 129]]),
                                                 start=(k == 0), stop=(k == KC - 1)),
                                       waits=(ewait + vtok_ready + bank_wait(pskey))
                                       if kcc == 0 else (),
                                       inc="TE" if kcc == KC - 1 else None)
                            vv = op("vector", lambda v, j=j, qt=qt, ps=ps: v.tensor_copy(
                                Ab[:, j, qt, 0:129], ps), waits=[("TE", t)], inc="VE")
                            bank_release(pskey, "VE", vv)
                        eguard[j] = [("TE", V("TE"))]
                    cmb = ARF[:, F_cmb:F_cmb + 16]
                    op("vector", lambda v: v.reciprocal(
                        cmb[:, 0:4],
                        bass.AP(ARF, F_ab + 128, [[ARF_N, 128], [132, 4], [1, 1]])),
                        waits=[("VE", V("VE"))], inc="VE")
                    op("vector", lambda v: v.reciprocal(
                        cmb[:, 4:8],
                        bass.AP(ARF, F_ab + 528 + 128, [[ARF_N, 128], [132, 4], [1, 1]])),
                        inc="VE")
                    op("vector", lambda v: v.tensor_tensor(
                        cmb[:, 8:12], cmb[:, 0:4],
                        bass.AP(ARF, F_ab + 128, [[ARF_N, 128], [132, 4], [1, 1]]),
                        OP.mult), waits=[("VE", V("VE"))], inc="VE")
                    op("vector", lambda v: v.tensor_tensor(
                        cmb[:, 12:16], cmb[:, 4:8],
                        bass.AP(ARF, F_ab + 528 + 128, [[ARF_N, 128], [132, 4], [1, 1]]),
                        OP.mult), inc="VE")
                    op("vector", lambda v: v.tensor_scalar(
                        cmb[:, 8:16], cmb[:, 8:16], -1.0, 2.0, OP.mult, OP.add),
                        waits=[("VE", V("VE"))], inc="VE")
                    op("vector", lambda v: v.tensor_tensor(
                        cmb[:, 0:4], cmb[:, 0:4], cmb[:, 8:12], OP.mult),
                        waits=[("VE", V("VE"))], inc="VE")
                    op("vector", lambda v: v.tensor_tensor(
                        cmb[:, 4:8], cmb[:, 4:8], cmb[:, 12:16], OP.mult),
                        waits=[("VE", V("VE"))], inc="VE")
                    op("vector", lambda v: v.tensor_scalar(
                        cmb[:, 4:8], cmb[:, 4:8], lamrep, -1.0, OP.mult, OP.mult),
                        waits=[("VE", V("VE"))], inc="VE")
                    for qt in range(4):
                        sq = ARF[:, F_sq:F_sq + 8]
                        op("vector", lambda v, qt=qt: v.tensor_scalar(
                            Ab[:, 0, qt, 0:128], Ab[:, 0, qt, 0:128],
                            cmb[:, qt:qt + 1], None, OP.mult),
                            waits=[("VE", V("VE"))], inc="VE")
                        op("vector", lambda v, qt=qt: v.scalar_tensor_tensor(
                            Ab[:, 0, qt, 0:128], Ab[:, 1, qt, 0:128],
                            cmb[:, 4 + qt:5 + qt], Ab[:, 0, qt, 0:128], OP.mult, OP.add),
                            waits=[("VE", V("VE"))], inc="VE")
                        op("vector", lambda v, qt=qt: v.bn_stats(
                            sq[:, 0:6], Ab[:, 0, qt, 0:128]),
                            waits=[("VE", V("VE"))], inc="VE")
                        op("vector", lambda v: v.bn_aggr(sq[:, 6:8], sq[:, 0:6]),
                           waits=[("VE", V("VE"))], inc="VE")
                        op("vector", lambda v: v.tensor_scalar(
                            sq[:, 0:1], sq[:, 6:7], sq[:, 6:7], None, OP.mult),
                            waits=[("VE", V("VE"))], inc="VE")
                        op("vector", lambda v: v.tensor_tensor(
                            sq[:, 0:1], sq[:, 0:1], sq[:, 7:8], OP.add),
                            waits=[("VE", V("VE"))], inc="VE")
                        op("scalar", lambda s: s.activation(
                            sq[:, 1:2], sq[:, 0:1], AF.Ln, bias=ARF[:, F_eps:F_eps + 1]),
                            waits=[("VE", V("VE"))], inc="AC")
                        op("scalar", lambda s: s.activation(
                            sq[:, 1:2], sq[:, 1:2], AF.Exp, scale=-0.5), inc="AC")
                        op("vector", lambda v, qt=qt, sl=wslot: v.tensor_scalar(
                            attq[:, sl, qt, :], Ab[:, 0, qt, 0:128], sq[:, 1:2],
                            None, OP.mult),
                            waits=[("AC", V("AC"))] + list(attq_g[wslot]), inc="VE")
                    atr = dma("sync",
                              nap(A_attnT + hp * 2048 + q0, [[128, 4], [1, 128]]),
                              ARN[:, A_attq + wslot * 512:A_attq + (wslot + 1) * 512],
                              waits=[("VE", V("VE"))] + list(attnT_guard),
                              transpose=True)
                    attq_g[wslot] = [("dS", atr)]

            attnT_done = V("dS")
            proj_te_done[0] = V("TE")
            ofeat = nview(A_E0, 8, 2048)
            for nt in range(EC):
                wsl = nt % 2
                wo_t = nview(A_wo + wsl * 1024, 8, 128)
                dma("gpsimd", wo_t,
                    bass.AP(wo_d, nt * 128, [[E, 128], [128 * E, 8], [1, 128]]),
                    waits=list(wo_g[wsl]))
                wo_flat = ARN[:, A_wo + wsl * 1024:A_wo + (wsl + 1) * 1024]
                sc = op("vector", lambda v, w=wo_flat: v.tensor_scalar(
                    w, w, ARF[:, F_wf:F_wf + 1], None, OP.mult),
                    waits=[("dG", V("dG"))], inc="VE")
                wo_mm_last = None
                for t4 in range(4):
                    pskey = ("oproj", t4 % 2)
                    ps = pb[t4 % 2][:, 0:512]
                    t = None
                    for kcc in range(EC):
                        t = op("tensor", lambda t_, k=kcc, t4=t4, ps=ps, w=wo_t: t_.matmul(
                            ps, w[:, k, :],
                            ARN[:, A_attnT + k * 2048 + t4 * 512:A_attnT + k * 2048 + (t4 + 1) * 512],
                            start=(k == 0), stop=(k == EC - 1)),
                            waits=([("VE", sc), ("dS", attnT_done)] + bank_wait(pskey))
                            if kcc == 0 else (),
                            inc="TE" if kcc == EC - 1 else None)
                    a = op("scalar", lambda s, nt=nt, t4=t4, ps=ps: s.activation(
                        ofeat[:, nt, t4 * 512:(t4 + 1) * 512], ps, AF.Copy),
                        waits=[("TE", t)], inc="AC")
                    bank_release(pskey, "AC", a)
                wo_g[wsl] = [("TE", V("TE")), ("VE", sc)]
            ofeat_ac = V("AC")
            outproj_te_done[0] = V("TE")
            for t4 in range(4):
                tr = None
                og4 = list(otok_g[0]) + (hsg[0] + hsg[1] if t4 == 0 else [])
                for nt in range(EC):
                    tr = dma("sync",
                             nap(O_hst + nt * 128, [[1024, 4], [1, 128]]),
                             ofeat[:, nt, t4 * 512:(t4 + 1) * 512],
                             waits=[("AC", ofeat_ac)] + og4,
                             transpose=True)
                vvs = []
                for ii in range(4):
                    i = t4 * 4 + ii
                    otok = nview(O_hst, 4, 1024)[:, ii, :]
                    xt = fview(F_xt, 2, 1024)[:, ii % 2, :]
                    ldx = dma("sync", xt,
                              bass.AP(x_d, (b * S + i * 128) * E, [[1024, 128], [1, 1024]]),
                              waits=list(xtg[ii % 2]))
                    op("vector", lambda v, o=otok, g=nview(vro(2), 1024): v.tensor_tensor(
                        o, o, g, OP.mult),
                        waits=[("dS", tr)], inc="VE")
                    vv = op("vector", lambda v, o=otok, x=xt: v.tensor_tensor(
                        x, x, o, OP.add), waits=[("dS", ldx), ("VE", V("VE"))], inc="VE")
                    vvs.append(vv)
                    st_d = dma("sync",
                               bass.AP(x1_d, (b * S + i * 128) * E, [[1024, 128], [1, 1024]]),
                               xt, waits=[("VE", vv)])
                    xtg[ii % 2] = [("dS", st_d)]
                otok_g[0] = [("VE", vvs[-1])]
            x1_done = V("dS")
            eregion_guard = [("dS", x1_done)]

            xqT = nview(B_xqT, 8, 512)
            sg = nview(B_sgmqT, 22, 512)
            mid = nview(B_mid, 22, 512)
            def emit_stage1(tb):
                for ii in range(4):
                    i = tb * 4 + ii
                    h2 = nview(B_h2, 1024)
                    hdone = ln_tile(b, i, x1_d, vro(4), vro(3), h2, ii % 2,
                                    og=h2_g[0])
                    xqv = nview(B_xq, 1024)
                    qd, qsrc = quant_generic(h2, 1024, 64, xqv,
                                             [("VE", hdone)] + list(xq_g[0]))
                    h2_g[0] = qsrc
                    qe = "scalar" if ii % 2 else "sync"
                    xtr = dma(qe, nap(B_xqT + ii * 128, [[512, 8], [1, 128]]), xqv,
                              waits=[("VE", qd), ("TE", upmm_te[0])], transpose=True)
                    xq_g[0] = [("dA" if ii % 2 else "dS", xtr)]
                return [("dS", V("dS")), ("dA", V("dA"))]

            xq_done = emit_stage1(0)
            mtok4_g = [[]]
            stg3_t1g = [[]]
            mqsl_g = {0: [], 1: list(dfetr_g[0])}
            for tb in range(NFTB):
                sg_ac = {}
                mid_ve = {}
                groups = [list(range(g0, min(IC, g0 + 4))) for g0 in range(0, IC, 4)]
                for phase in range(2):
                    for gi, nts in enumerate(groups):
                        slot = gi % 2
                        lw = None
                        for ni, nt in enumerate(nts):
                            lw = dma("sync", nview(B_wgu + slot * 4096 + ni * 1024, 8, 128),
                                     bass.AP(wgT_d if phase == 0 else wuT_d, nt * 128,
                                             [[I, 128], [128 * I, 8], [1, 128]]),
                                     waits=list(wgu_g[slot]) if ni == 0 else ())
                        for ni, nt in enumerate(nts):
                            w_t = nview(B_wgu + slot * 4096 + ni * 1024, 8, 128)
                            pskey = ("ffn_gu", phase, nt % 2)
                            ps = pb[1 + phase][:, (nt % 2) * 512:(nt % 2) * 512 + 512]
                            t = None
                            for kcc in range(EC):
                                t = op("tensor", lambda t_, k=kcc, ps=ps, w=w_t: t_.matmul(
                                    ps, w[:, k, :], xqT[:, k, :],
                                    start=(k == 0), stop=(k == EC - 1)),
                                    waits=([("dS", lw)] + list(xq_done)
                                           + bank_wait(pskey)) if kcc == 0 else (),
                                    inc="TE" if kcc == EC - 1 else None)
                            sgf = ARN[:, B_sgmqT + nt * 512:B_sgmqT + nt * 512 + 512]
                            mdf = ARN[:, B_mid + nt * 512:B_mid + nt * 512 + 512]
                            if phase == 0:
                                a = op("scalar", lambda s, o=sgf, ps=ps: s.activation(
                                    o, ps, AF.Silu), waits=[("TE", t)], inc="AC")
                                bank_release(pskey, "AC", a)
                                sg_ac[nt] = a
                            else:
                                vv = op("vector", lambda v, o=mdf, g2=sgf, ps=ps:
                                        v.tensor_tensor(o, g2, ps, OP.mult),
                                        waits=[("TE", t), ("AC", sg_ac[nt])]
                                        + list(midtr_g[0]), inc="VE")
                                bank_release(pskey, "VE", vv)
                                mid_ve[nt] = vv
                        wgu_g[slot] = [("TE", V("TE"))]
                upmm_te[0] = V("TE")
                mid_all = V("VE")
                mqT_tr = []
                trtail = {}
                for nt in range(IC):
                    qe, qs = ("scalar", "dA") if nt % 2 else ("sync", "dS")
                    trtail[qs] = dma(
                        qe,
                        bass.AP(ARN, B_T1 + nt * 128,
                                [[ARN_N, 128], [2816, 4], [1, 128]]),
                        ARN[:, B_mid + nt * 512:B_mid + nt * 512 + 512],
                        waits=[("VE", mid_ve[nt])]
                        + (mtok4_g[0] if nt < 2 else []), transpose=True)
                trs_done = [(k, v) for k, v in trtail.items()]
                for tile in range(4):
                    sl = tile % 2
                    mqo = B_mq if sl == 0 else B_dfe
                    mtk = ARN[:, B_T1 + tile * 2816:B_T1 + (tile + 1) * 2816]
                    qd, qsrc = quant_generic(mtk, 2816, 176, nview(mqo, 2816),
                                             list(trs_done),
                                             t1o=B_mtok,
                                             t1war=stg3_t1g[0],
                                             out_war=mqsl_g[sl],
                                             tmpo=(B_mtok + 2816, B_wgu,
                                                   B_wgu + 4096))
                    stg3_t1g[0] = [("VE", qd)]
                    if tile == 0:
                        mtok4_g[0] = []
                    mtok4_g[0] = mtok4_g[0] + qsrc
                    qe, qs = ("scalar", "dA") if tile % 2 else ("sync", "dS")
                    mqT_tr.append((qs, dma(
                        qe, nap(B_sgmqT + tile * 128, [[512, 22], [1, 128]]),
                        nview(mqo, 2816), waits=[("VE", qd)], transpose=True)))
                    mqsl_g[sl] = [mqT_tr[-1]]
                for _sl in (0, 1):
                    wgu_g[_sl] = wgu_g[_sl] + [("VE", stg3_t1g[0][0][1])]
                if tb < NFTB - 1:
                    xq_done = emit_stage1(tb + 1)
                dfe_ac = V("AC")
                for ntE in range(EC):
                    slot = ntE % 2
                    wd_t = nview(B_wd2 if slot else B_wd, 22, 128)
                    lw = dma("sync", wd_t,
                             bass.AP(wdT_d, ntE * 128, [[E, 128], [128 * E, 22], [1, 128]]),
                             waits=list(wd_g[slot]) if wd_g[slot]
                             else [("TE", V("TE"))])
                    pskey = ("ffn_d", ntE % 2)
                    ps = pb[3][:, (ntE % 2) * 512:(ntE % 2) * 512 + 512]
                    t = None
                    for kcc in range(IC):
                        t = op("tensor", lambda t_, k=kcc, ps=ps, w=wd_t: t_.matmul(
                            ps, w[:, k, :],
                            nap(B_sgmqT + k * 512, [[1, 512]]),
                            start=(k == 0), stop=(k == IC - 1)),
                            waits=([("dS", lw), mqT_tr[-1], mqT_tr[-2]]
                                   + bank_wait(pskey))
                            if kcc == 0 else (),
                            inc="TE" if kcc == IC - 1 else None)
                    wd_g[slot] = [("TE", t)]
                    a = op("scalar", lambda s, ntE=ntE, ps=ps: s.activation(
                        nview(B_dfe, 8, 512)[:, ntE, :], ps, AF.Copy),
                        waits=[("TE", t)] + list(dfetr_g[0]), inc="AC")
                    bank_release(pskey, "AC", a)
                dfe_done = V("AC")
                tr = None
                for nt in range(EC):
                    tr = dma("sync",
                             nap(O_hst + nt * 128, [[1024, 4], [1, 128]]),
                             nview(B_dfe, 8, 512)[:, nt, :],
                             waits=[("AC", dfe_done)] + list(otok_g[0]),
                             transpose=True)
                dfetr_g[0] = [("dS", tr)]
                mqsl_g[1] = mqsl_g[1] + [("dS", tr)]
                vvs = []
                for ii in range(4):
                    i = tb * 4 + ii
                    dtok = nview(O_hst, 4, 1024)[:, ii, :]
                    xt = fview(F_xt, 2, 1024)[:, ii % 2, :]
                    ldx = dma("sync", xt,
                              bass.AP(x1_d, (b * S + i * 128) * E, [[1024, 128], [1, 1024]]),
                              waits=list(xtg[ii % 2]))
                    op("vector", lambda v, o=dtok, g=nview(vro(5), 1024): v.tensor_tensor(
                        o, o, g, OP.mult),
                        waits=[("dS", tr)], inc="VE")
                    vv = op("vector", lambda v, o=dtok, x=xt: v.tensor_tensor(
                        x, x, o, OP.add), waits=[("dS", ldx), ("VE", V("VE"))], inc="VE")
                    vvs.append(vv)
                    st_d = dma("sync",
                               bass.AP(y_d, (b * S + i * 128) * E, [[1024, 128], [1, 1024]]),
                               xt, waits=[("VE", vv)])
                    xtg[ii % 2] = [("dS", st_d)]
                otok_g[0] = [("VE", vvs[-1])]
            batch_guard = [(x, V(x)) for x in ("TE", "AC", "VE", "dS", "dG", "PL")]
            hT_guard = list(batch_guard)
            attnT_guard = list(batch_guard)
            eregion_guard = list(batch_guard)
            for _sl in (0, 1):
                wqkv_g[_sl] = list(batch_guard)
                wo_g[_sl] = list(batch_guard)
                vfe_g[_sl] = list(batch_guard)
                attq_g[_sl] = list(batch_guard)
                wd_g[_sl] = list(batch_guard)

        op("sync", lambda e: e.nop(),
           waits=[(s, V(s)) for s in SEM_NAMES])

        with nc.Block() as block:
            def replay(name):
                def emit(eng):
                    last = {}
                    for fn, waits, inc, k, cum in em.ev[name]:
                        for (s, v) in waits:
                            if last.get(s, 0) < v:
                                eng.wait_ge(SEM[s], v)
                                last[s] = v
                        inst = fn(eng)
                        if inc is not None:
                            inst.then_inc(SEM[inc], k)
                            if inc in em.chk and cum in em.chk[inc] \
                                    and last.get(inc, 0) < cum:
                                eng.wait_ge(SEM[inc], cum)
                                last[inc] = cum
                return emit

            block.sync(replay("sync"))
            block.tensor(replay("tensor"))
            block.vector(replay("vector"))
            block.scalar(replay("scalar"))
            block.gpsimd(replay("gpsimd"))

    n_inst = sum(len(v) for v in em.ev.values())
    return nc, n_inst


_CACHE = {}


def _get_nc():
    if "nc" not in _CACHE:
        _CACHE["nc"] = build_program()[0]
    return _CACHE["nc"]


def kernel(**inputs):
    def f32c(a):
        return np.ascontiguousarray(np.asarray(a, dtype=np.float32))

    x = f32c(inputs["L0_attn_query"])
    cv = f32c(inputs["L0_c"])
    shared = {
        "wq": f32c(inputs["L0_attn_Wq"]),
        "wk": f32c(inputs["L0_attn_Wk"]),
        "wv": f32c(inputs["L0_attn_Wv"]),
        "wo": f32c(inputs["L0_attn_Wo"]),
        "wg": f32c(inputs["L0_ffn_w_gate"]),
        "wu": f32c(inputs["L0_ffn_w_up"]),
        "wd": f32c(inputs["L0_ffn_w_down"]),
        "wmod": f32c(inputs["L0_wmod"]),
        "lq1": f32c(inputs["L0_attn_lq1"]).reshape(1, D),
        "lk1": f32c(inputs["L0_attn_lk1"]).reshape(1, D),
        "lq2": f32c(inputs["L0_attn_lq2"]).reshape(1, D),
        "lk2": f32c(inputs["L0_attn_lk2"]).reshape(1, D),
        "subw": f32c(inputs["L0_attn_subln_w"]).reshape(DV, 1),
    }
    in_maps = []
    for c in range(8):
        m = dict(shared)
        m["x"] = np.ascontiguousarray(x[2 * c:2 * c + 2])
        m["cv"] = np.ascontiguousarray(cv[2 * c:2 * c + 2])
        in_maps.append(m)
    nc = _get_nc()
    res = run_bass_kernel_spmd(nc, in_maps, core_ids=list(range(8)))
    out = np.concatenate([np.asarray(res.results[c]["y"], dtype=np.float32)
                          for c in range(8)], axis=0)
    return out.reshape(16, S, E)


if __name__ == "__main__":
    nc, n = build_program()
    print("built ok, instructions:", n)
